# version 12
# speedup vs baseline: 1.9906x; 1.0977x over previous
import numpy as np
import ml_dtypes
import concourse.bass as bass
import concourse.mybir as mybir
from concourse.bass_core_rust import IndirectOffsetOnAxis
from concourse import dve_ops

F32 = mybir.dt.float32
BF16 = mybir.dt.bfloat16
I32 = mybir.dt.int32
AF = mybir.ActivationFunctionType
ALU = mybir.AluOpType
AX = mybir.AxisListType
BF = ml_dtypes.bfloat16

NCORES = 8
D = 1024
DC = 8
H = 16
E = 8
HID = 2048
HC = 16
OOB = 131072
EPS = 1e-5


class Sem:
    def __init__(self, nc, name):
        self.h = nc.alloc_semaphore(name)
        self.n = 0


def build_nc(T=2048, CAP=2048, dbg=False, part="all"):
    TT = T // 128
    N = NCORES * T
    S = N // 128
    CT = CAP // 128
    SP = T // 128
    NSUP = T // 512
    TB = min(1024, CAP)
    NTB = CAP // TB
    NG = H // 2

    nc = bass.Bass("TRN2", target_bir_lowering=False, debug=False,
                   dynamic_dma_scratch_size=32768)

    def din(name, shape, dt=F32):
        return nc.dram_tensor(name, list(shape), dt, kind="ExternalInput").ap()

    xT = din("xT", [D, T])
    xtm = din("xtm", [T, D])
    wqTg = din("wqTg", [NG, D, 128])
    wkTg = din("wkTg", [NG, D, 128])
    wvTg = din("wvTg", [NG, D, 128])
    woT = din("woT", [D, D])
    gammac = din("gammac", [128, DC])
    betac = din("betac", [128, DC])
    biasdT = din("biasdT", [H, 128, 128])
    biasoT = din("biasoT", [H, 128, 128])
    relin = din("relin", [H, 129])
    rel128b = din("rel128b", [128, H])
    rwT = din("rwT", [D, E])
    rbb = din("rbb", [128, E])
    We = din("We", [D, HID])
    Ve = din("Ve", [D, HID])
    WoutE = din("WoutE", [HID, D])
    eidcol = din("eidcol", [128, 1])
    out = nc.dram_tensor("out", [T, D], F32, kind="ExternalOutput").ap()

    def dint(name, shape, dt):
        return nc.dram_tensor(name, list(shape), dt, kind="Internal").ap()

    if part == "a":
        ag_in = nc.dram_tensor("ag_in", [T, D], BF16, kind="ExternalOutput").ap()
        dx_dram = nc.dram_tensor("dx_dram", [T, D], BF16, kind="ExternalOutput").ap()
        lg_in = nc.dram_tensor("lg_in", [T, E], F32, kind="ExternalOutput").ap()
        x1_full = None
        lg_full = None
    elif part == "b":
        x1_full = din("x1_full", [N, D], BF16)
        lg_full = din("lg_full", [N, E], F32)
        ag_in = None
        dx_dram = None
        lg_in = None
    elif part == "c":
        ag_in = din("ag_in", [T, D], BF16)
        dx_dram = din("dx_dram", [T, D], BF16)
        x1_full = None
        lg_full = None
        lg_in = None
    else:
        ag_in = dint("ag_in", [T, D], BF16)
        x1_full = dint("x1_full", [N, D], BF16)
        dx_dram = dint("dx_dram", [T, D], BF16)
        lg_in = dint("lg_in", [T, E], F32)
        lg_full = dint("lg_full", [N, E], F32)
    gw_dram = dint("gw_dram", [CAP + 128, 128], F32)
    k1_dram = dint("k1_dram", [128, S], F32)
    k2_dram = dint("k2_dram", [128, S], F32)
    idx1_dram = dint("idx1_dram", [128, S], I32)
    idx2_dram = dint("idx2_dram", [128, S], I32)
    xin_dram = dint("xin_dram", [CAP, D], BF16)
    if part == "b":
        Y_in = nc.dram_tensor("Y_in", [CAP, D], BF16, kind="ExternalOutput").ap()
        Y_full = None
        ixst_out = nc.dram_tensor("ixst_out", [128, 2 * (T // 128)], I32, kind="ExternalOutput").ap()
        kst_out = nc.dram_tensor("kst_out", [128, 2 * (T // 128)], F32, kind="ExternalOutput").ap()
    elif part == "c":
        Y_in = None
        Y_full = din("Y_full", [E * CAP, D], BF16)
        ixst_in = din("ixst_in", [128, 2 * (T // 128)], I32)
        kst_in = din("kst_in", [128, 2 * (T // 128)], F32)
    else:
        Y_in = dint("Y_in", [CAP, D], BF16)
        Y_full = dint("Y_full", [E * CAP, D], BF16)

    ident_bf = nc.inline_tensor(np.eye(128, dtype=BF), "c_ident").ap()
    ones128_bf = nc.inline_tensor(np.ones((128, 1), dtype=BF), "c_o128").ap()
    ones1_bf = nc.inline_tensor(np.ones((1, 128), dtype=BF), "c_o1").ap()
    ones1_f32 = nc.inline_tensor(np.ones((1, 128), dtype=np.float32), "c_o1f").ap()
    ultri_bf = nc.inline_tensor(
        np.triu(np.ones((128, 128), np.float32), 1).astype(BF), "c_ultri").ap()
    bd2_np = np.zeros((128, 2), dtype=BF)
    bd2_np[0:64, 0] = 1
    bd2_np[64:128, 1] = 1
    bd2_bf = nc.inline_tensor(bd2_np, "c_bd2").ap()
    utri_incl_bf = nc.inline_tensor(
        np.triu(np.ones((128, 128), np.float32), 0).astype(BF), "c_utinc").ap()
    tw_np = np.zeros((128, 2 * S + 128), np.float32)
    tw_np[:, 0:2 * S:2] = (np.arange(S)[None, :] * 128 + np.arange(128)[:, None])
    tw_c = nc.inline_tensor(tw_np, "c_tw").ap()
    colid_c = nc.inline_tensor(
        np.tile(np.arange(S, dtype=np.float32)[None, :], (128, 1)), "c_colid").ap()
    gwi_np = np.zeros((128, 2 * (CT + 1)), np.float32)
    gwi_c = nc.inline_tensor(gwi_np, "c_gwi").ap()
    epsc_np = np.zeros((128, 2), np.float32)
    epsc_np[:, 0] = EPS
    epsc_np[:, 1] = 1e-30
    eps_c = nc.inline_tensor(epsc_np, "c_eps").ap()

    def sb(name, shape, dt):
        return nc.alloc_sbuf_tensor(name, list(shape), dt)

    c_id = sb("c_id", [128, 128], BF16)
    c_o128 = sb("c_o128s", [128, 1], BF16)
    c_o1 = sb("c_o1s", [1, 128], BF16)
    c_o1f = sb("c_o1fs", [1, 128], F32)
    c_ut = sb("c_uts", [128, 128], BF16)
    c_bd2 = sb("c_bd2s", [128, 2], BF16)
    c_uti = sb("c_utis", [128, 128], BF16)
    c_tw = sb("c_tws", [128, 2 * S + 128], F32)
    c_gwi = sb("c_gwis", [128, 2 * (CT + 1)], F32)
    c_cid2 = sb("c_cid2", [128, S], F32)
    epsc = sb("epsc", [128, 2], F32)
    gam_s = sb("gam_s", [128, DC], F32)
    bet_s = sb("bet_s", [128, DC], F32)
    rbb_s = sb("rbb_s", [128, E], F32)
    eid_s = sb("eid_s", [128, 1], F32)
    rel_s = sb("rel_s", [H, 129], F32)
    r128_s = sb("r128_s", [128, H], F32)
    relsm = sb("relsm", [H, 2], F32)
    mbcol = [sb(f"mbc{g}", [2, 1], F32) for g in range(NG)]
    ccol = [sb(f"ccc{g}", [2, 1], F32) for g in range(NG)]
    biasd_s = sb("biasd_s", [128, H * 128], BF16)
    biaso_s = sb("biaso_s", [128, H * 128], BF16)
    bstage = sb("bstage", [128, 128], F32)
    rw_s = sb("rw_s", [128, DC * E], BF16)
    smc = sb("smc", [128, 16], F32)
    ix1_s = sb("ix1_s", [128, S], I32)
    ix2_s = sb("ix2_s", [128, S], I32)
    off_s = sb("off_s", [128, S], I32)
    tw_s = sb("tw_s", [128, 2 * S + 128], F32)
    gwt_s = sb("gwt_s", [128, 2 * CT], F32)
    ixst = sb("ixst", [128, 32], I32)
    drec = sb("drec", [128, 32], F32)
    kst = sb("kst", [128, 32], F32)
    gcol_i = sb("gcol_i", [128, CT], I32)

    WA = max(8 * T + 256, HC * TB, SP * 1024)
    WB = max(8 * T, CT * 1024, 8 * CAP, SP * 1024)
    WY = max(8 * T, HC * 1024, 4 * 4096)
    bigA = sb("bigA", [128, WA], BF16)
    bigB = sb("bigB", [128, WB], BF16)
    scr = sb("scr", [128, 8192], F32)
    wb = sb("wb", [128, 8 * 1024], BF16)
    atb = sb("atb", [128, max(16 * 512, 4 * T)], BF16)
    ynT = sb("ynT", [128, WY], BF16)
    sqb = sb("sqb", [128, 2048], BF16)

    psA = nc.alloc_psum_tensor("psA", [128, 2048], F32)
    psB = nc.alloc_psum_tensor("psB", [128, 2048], F32)

    xbf = lambda dc: bigA[:, dc * T:(dc + 1) * T]
    xnbf = lambda dc: bigB[:, dc * T:(dc + 1) * T]
    qx = lambda hl: bigA[0:65, hl * T:(hl + 1) * T]
    kx = lambda hl: bigA[0:65, (2 + hl) * T:(3 + hl) * T]
    VX0 = 4 * T
    vxA = lambda tt: bigA[:, VX0 + tt * 65:VX0 + (tt + 1) * 65]
    VX1 = 4 * T + TT * 65
    vxB = lambda tt: bigA[:, VX1 + tt * 65:VX1 + (tt + 1) * 65]
    QPK0 = 4 * T + 2 * TT * 65
    qpk = bigA[:, QPK0:QPK0 + T]
    kpk = bigA[:, QPK0 + T:QPK0 + 2 * T]
    stgw = min(T, 2048)
    stg = bigA[:, QPK0 + 2 * T:QPK0 + 2 * T + stgw]
    stg2 = sb("stg2", [128, 2048], BF16)
    assert QPK0 + 2 * T + stgw <= WA
    attnT = lambda jj: atb[:, jj * 512:(jj + 1) * 512]

    ENGS = ("sync", "tensor", "vector", "scalar", "gpsimd")
    sched = {e: [] for e in ENGS}
    sem = {n: Sem(nc, "s_" + n) for n in
           ("dve", "act", "pe", "d_c", "d_x0", "d_x1", "d_w", "d_wa", "d_wb",
            "d_g", "d_rw", "d_t", "d_ta", "d_tb", "d_oa", "d_ob",
            "d_r", "d_ks", "d_gw", "d_sc", "d_gc", "d_gd", "d_ix",
            "d_s1", "d_s2", "d_rc", "d_wo", "coll")}

    def emit(eng, build_fn, inc=None, inck=1, waits=()):
        wcap = [(s.h, v) for (s, v) in waits if v > 0]
        tgt = None
        if inc is not None:
            inc.n += inck
            tgt = inc.h

        def closure(e, bf=build_fn, wcap=wcap, tgt=tgt, k=inck):
            for (hh, vv) in wcap:
                e.wait_ge(hh, vv)
            ins = bf(e)
            if tgt is not None:
                ins.then_inc(tgt, k)
        sched[eng].append(closure)

    def waitonly(eng, waits):
        wcap = [(s.h, v) for (s, v) in waits if v > 0]

        def closure(e, wcap=wcap):
            for (hh, vv) in wcap:
                e.wait_ge(hh, vv)
        sched[eng].append(closure)

    def dve(build_fn, waits=()):
        emit("vector", build_fn, inc=sem["dve"],
             waits=((sem["dve"], sem["dve"].n),) + tuple(waits))
        return sem["dve"].n

    def act(build_fn, waits=()):
        emit("scalar", build_fn, inc=sem["act"],
             waits=((sem["act"], sem["act"].n),) + tuple(waits))
        return sem["act"].n

    def pe(build_fn, waits=(), inc=True):
        emit("tensor", build_fn, inc=sem["pe"] if inc else None, waits=waits)
        return sem["pe"].n

    def dmaq(eng, s, build_fn, waits=()):
        emit(eng, build_fn, inc=sem[s], inck=16, waits=waits)
        return sem[s].n

    W = lambda s: (sem[s], sem[s].n)

    _bccache = {}

    def bc(e, v):
        if v not in _bccache:
            _bccache[v] = e.snap(v)
        return _bccache[v]

    for (dst, src) in [(c_id, ident_bf), (c_o128, ones128_bf), (c_o1, ones1_bf), (c_o1f, ones1_f32),
                       (c_ut, ultri_bf), (c_bd2, bd2_bf),                        (c_uti, utri_incl_bf), (c_tw, tw_c), (c_gwi, gwi_c), (c_cid2, colid_c), (epsc, eps_c),
                       (gam_s, gammac), (bet_s, betac), (rbb_s, rbb),
                       (eid_s, eidcol), (rel_s, relin), (r128_s, rel128b)]:
        dmaq("sync", "d_c", lambda e, d=dst, s2=src: e.dma_start(out=d[:], in_=s2[:]))
    dmaq("gpsimd", "d_rw", lambda e: e.dma_start(
        out=rw_s[:].rearrange("p (c e) -> p c e", e=E),
        in_=rwT.rearrange("(c p) e -> p c e", p=128)))
    RWLD = sem["d_rw"].n

    do_a = part in ("all", "a")
    do_b = part in ("all", "b")
    do_c = part in ("all", "c")
    do_coll = part == "all"
    AGX1 = AGLG = AGY = 0
    KSALL = 0
    IXALL = 0
    if do_a:
        dve(lambda e: e.reduce_max(relsm[:, 0:1], rel_s[:], axis=AX.X), waits=[W("d_c")])
        dve(lambda e: e.tensor_tensor(relsm[:, 0:1], relsm[:, 0:1], rel_s[:, 128:129],
                                      ALU.subtract))
        RELSM = dve(lambda e: e.tensor_tensor(relsm[:, 1:2], rel_s[:, 0:1],
                                              rel_s[:, 128:129], ALU.subtract))
        for g in range(NG):
            dmaq("gpsimd", "d_g", lambda e, g=g: e.dma_start(
                out=mbcol[g][:], in_=relsm[2 * g:2 * g + 2, 0:1]),
                waits=[(sem["dve"], RELSM)])
            dmaq("gpsimd", "d_g", lambda e, g=g: e.dma_start(
                out=ccol[g][:], in_=relsm[2 * g:2 * g + 2, 1:2]))
        MBCC = sem["d_g"].n

        BDL = dmaq("sync", "d_rc", lambda e: e.dma_start(
            out=scr[:, 0:H * 128].rearrange("p (h i) -> p h i", i=128),
            in_=biasdT.rearrange("h p i -> p h i")))
        BOL = dmaq("gpsimd", "d_g", lambda e: e.dma_start(
            out=scr[:, 2048:2048 + H * 128].rearrange("p (h i) -> p h i", i=128),
            in_=biasoT.rearrange("h p i -> p h i")))
        MBCC = sem["d_g"].n
        for hh in range(H):
            dve(lambda e, hh=hh: e.tensor_scalar(
                biasd_s[:, hh * 128:(hh + 1) * 128], scr[:, hh * 128:(hh + 1) * 128],
                r128_s[:, hh:hh + 1], None, ALU.subtract),
                waits=[(sem["d_rc"], BDL)] if hh == 0 else [])
            dve(lambda e, hh=hh: e.tensor_scalar(
                biaso_s[:, hh * 128:(hh + 1) * 128],
                scr[:, 2048 + hh * 128:2048 + (hh + 1) * 128],
                r128_s[:, hh:hh + 1], None, ALU.subtract),
                waits=[(sem["d_g"], BOL)] if hh == 0 else [])

        xld = [scr[:, 4096 + i * 2048:4096 + i * 2048 + T] for i in range(2)]
        cast_done = []
        m2_done = []
        for dc in range(DC):
            slot = dc % 2
            xs = "d_x0" if slot == 0 else "d_x1"
            w = [(sem["dve"], cast_done[dc - 2])] if dc >= 2 else []
            XW = dmaq("sync", xs, lambda e, dc=dc, slot=slot: e.dma_start(
                out=xld[slot][:], in_=xT[dc * 128:(dc + 1) * 128, :]), waits=w)
            cast_done.append(dve(lambda e, dc=dc, slot=slot: e.tensor_copy(
                xbf(dc), xld[slot][:]), waits=[(sem[xs], XW)]))
            sqw = [W("dve")] + ([(sem["pe"], m2_done[-1])] if dc > 0 else [])
            SQA = act(lambda e, dc=dc: e.activation(sqb[:, 0:T], xbf(dc), AF.Square),
                      waits=sqw)
            for sl in range(T // 512):
                pe(lambda e, dc=dc, sl=sl: e.matmul(
                    psA[0:1, sl * 512:(sl + 1) * 512], c_o128[:],
                    bigA[:, dc * T + sl * 512:dc * T + (sl + 1) * 512],
                    start=(dc == 0), stop=(dc == DC - 1)),
                   waits=[(sem["dve"], cast_done[-1])] if sl == 0 else [], inc=False)
                m2_done.append(pe(lambda e, sl=sl, dc=dc: e.matmul(
                    psB[0:1, sl * 512:(sl + 1) * 512], c_o128[:],
                    sqb[:, sl * 512:(sl + 1) * 512],
                    start=(dc == 0), stop=(dc == DC - 1)),
                   waits=[(sem["act"], SQA)] if sl == 0 else [],
                   inc=(sl == T // 512 - 1)))
        PE_ST = sem["pe"].n

        rmu = scr[0:1, 4096:4096 + T]
        rv_ = scr[0:1, 6144:6144 + T]
        dve(lambda e: e.tensor_scalar(rmu, psA[0:1, 0:T], 1.0 / D, None, ALU.mult),
            waits=[(sem["pe"], PE_ST)])
        dve(lambda e: e.tensor_scalar(rv_, psB[0:1, 0:T], 1.0 / D, None, ALU.mult))
        dve(lambda e: e.tensor_tensor(psB[0:1, 0:T], rmu, rmu, ALU.mult))
        VARD = dve(lambda e: e.tensor_tensor(rv_, rv_, psB[0:1, 0:T], ALU.subtract))
        act(lambda e: e.activation(psB[0:1, 0:T], rv_, AF.Ln, bias=epsc[0:1, 0:1]), waits=[W("dve")])
        RSTA = act(lambda e: e.activation(rv_, psB[0:1, 0:T], AF.Exp, scale=-0.5))
        dve(lambda e: e.tensor_scalar(wb[0:1, 0:T], rmu, -1.0, None, ALU.mult),
            waits=[(sem["act"], RSTA)])
        ROWS_BF = dve(lambda e: e.tensor_copy(wb[0:1, T:2 * T], rv_))
        for sl in range(T // 512):
            pe(lambda e, sl=sl: e.matmul(
                psA[:, sl * 512:(sl + 1) * 512], c_o1[:],
                wb[0:1, sl * 512:(sl + 1) * 512], start=True, stop=True),
               waits=[(sem["dve"], ROWS_BF)] if sl == 0 else [], inc=False)
            pe(lambda e, sl=sl: e.matmul(
                psB[:, sl * 512:(sl + 1) * 512], c_o1[:],
                wb[0:1, T + sl * 512:T + (sl + 1) * 512], start=True, stop=True),
               inc=(sl == T // 512 - 1))
        PE_BC = sem["pe"].n
        act(lambda e: e.activation(scr[:, 4096:4096 + T], psA[:, 0:T], AF.Copy),
            waits=[(sem["pe"], PE_BC)])
        BCP = act(lambda e: e.activation(scr[:, 6144:6144 + T], psB[:, 0:T],
                                         AF.Copy))
        xn_done = []
        for dc in range(DC):
            dve(lambda e, dc=dc: e.tensor_tensor(scr[:, 0:T], xbf(dc),
                                                 scr[:, 4096:4096 + T], ALU.add),
                waits=[(sem["act"], BCP)] if dc == 0 else [])
            dve(lambda e: e.tensor_tensor(scr[:, 2048:2048 + T], scr[:, 0:T],
                                          scr[:, 6144:6144 + T], ALU.mult))
            xn_done.append(dve(lambda e, dc=dc: e.tensor_scalar(
                xnbf(dc), scr[:, 2048:2048 + T],
                gam_s[:, dc:dc + 1], bet_s[:, dc:dc + 1],
                ALU.mult, ALU.add)))
        XN_DONE = sem["dve"].n

        wbq = lambda dc: wb[:, dc * 128:(dc + 1) * 128]
        wbk = lambda dc: wb[:, 1024 + dc * 128:1024 + (dc + 1) * 128]
        wbv = lambda dc: wb[:, 2048 + dc * 128:2048 + (dc + 1) * 128]

        NSL = T // 512

        def flush_part(fl, sls, mult_order=None):
            for sl in sls:
                w = [(sem["dve"], fl["cast"][sl])]
                if sl == fl["sl_own"]:
                    w.append((sem["act"], fl["drain3"]))
                if fl["hl"] == 1 and sl == 0:
                    w.append((sem["act"], fl["h0drain3"]))
                fl["bc_pe"][sl] = pe(lambda e, sl=sl, hl=fl["hl"]: e.matmul(
                    psB[hl * 64:hl * 64 + 64, sl * 512:(sl + 1) * 512],
                    (c_o1[0:1, 0:64] if hl == 0 else c_uti[32:33, 64:128]),
                    sqb[32 * hl:32 * hl + 1, sl * 512:(sl + 1) * 512],
                    start=True, stop=True), waits=w, inc=True)
            if mult_order is not None:
                for sl in mult_order:
                    fl["mults"][sl] = dve(
                        lambda e, sl=sl, hl=fl["hl"], hg=fl["hg"]: e.tensor_tensor(
                            ynT[hl * 64:hl * 64 + 64,
                                hg * T + sl * 512:hg * T + (sl + 1) * 512],
                            ynT[hl * 64:hl * 64 + 64,
                                hg * T + sl * 512:hg * T + (sl + 1) * 512],
                            psB[hl * 64:hl * 64 + 64, sl * 512:(sl + 1) * 512],
                            ALU.mult),
                        waits=[(sem["pe"], fl["bc_pe"][sl])])
                fl["mlast"] = sem["dve"].n

        prev_vmm = None
        prev_flush = None
        prev_mults = None
        prev_mlast = None
        last_expall = None
        last_casts = None
        WOLD = None
        for g in range(NG):
            h0 = 2 * g
            for (wsrc, off) in ((wqTg, 0), (wkTg, 1024), (wvTg, 2048)):
                dmaq("gpsimd", "d_w", lambda e, wsrc=wsrc, off=off, g=g: e.dma_start(
                    out=wb[:, off:off + 1024].rearrange("p (c i) -> p c i", i=128),
                    in_=wsrc[g].rearrange("(c p) i -> p c i", p=128)),
                    waits=[(sem["pe"], prev_vmm if g > 0 else PE_BC)])
            WLD = sem["d_w"].n
            if g == 0:
                for dcw in range(5):
                    dmaq("gpsimd", "d_wo", lambda e, dcw=dcw: e.dma_start(
                        out=wb[:, 3072 + dcw * 1024:3072 + (dcw + 1) * 1024],
                        in_=woT[dcw * 128:(dcw + 1) * 128, :]),
                        waits=[(sem["pe"], PE_BC)] if dcw == 0 else [])
            for dc in range(DC):
                for sl in range(T // 512):
                    w = []
                    if dc == 0 and sl == 0:
                        w = [(sem["d_w"], WLD)]
                        if last_expall is not None:
                            w.append((sem["act"], last_expall))
                    if g == 0 and sl == 0:
                        w = w + [(sem["dve"], xn_done[dc])]
                        if dc == 0:
                            w.append((sem["act"], BCP))
                    pe(lambda e, dc=dc, sl=sl: e.matmul(
                        psA[:, sl * 512:(sl + 1) * 512], wbq(dc),
                        bigB[:, dc * T + sl * 512:dc * T + (sl + 1) * 512],
                        start=(dc == 0), stop=(dc == DC - 1)),
                       waits=w, inc=(dc == DC - 1 and sl == T // 512 - 1))
                if dc == 0 and prev_flush is not None:
                    p1 = [s for s in range(NSL - 1) if s != prev_flush["sl_own"]]
                    if prev_flush["sl_own"] < NSL - 1:
                        p1.append(prev_flush["sl_own"])
                    flush_part(prev_flush, p1)
                if dc == 1 and prev_flush is not None:
                    mo = sorted(range(NSL), key=lambda s: 0 if s == 0 else 1 + s)
                    flush_part(prev_flush, [NSL - 1], mult_order=mo)
                    prev_mults = prev_flush["mults"]
                    prev_mlast = prev_flush["mlast"]
                    prev_flush = None
            QMM = sem["pe"].n
            QPKd = act(lambda e: e.activation(qpk[:, 0:T], psA[:, 0:T], AF.Copy,
                                              scale=0.125),
                       waits=[(sem["pe"], QMM)])
            dve(lambda e: e.tensor_copy(qx(0)[0:64, :], qpk[0:64, 0:T]),
                waits=[(sem["act"], QPKd)])
            dve(lambda e: e.tensor_copy(qx(1)[0:64, :], qpk[64:128, 0:T]))
            SQQ = act(lambda e: e.activation(stg[:, 0:T], qpk[:, 0:T], AF.Square))
            for dc in range(DC):
                for sl in range(T // 512):
                    w = []
                    if dc == 0 and sl == 0:
                        w = [(sem["dve"], prev_mlast if prev_mlast is not None
                              else XN_DONE)]
                    pe(lambda e, dc=dc, sl=sl: e.matmul(
                        psB[:, sl * 512:(sl + 1) * 512], wbk(dc),
                        bigB[:, dc * T + sl * 512:dc * T + (sl + 1) * 512],
                        start=(dc == 0), stop=(dc == DC - 1)),
                       waits=w, inc=(dc == DC - 1 and sl == T // 512 - 1))
            KMM = sem["pe"].n
            KPKd = act(lambda e: e.activation(kpk[:, 0:T], psB[:, 0:T], AF.Copy),
                       waits=[(sem["pe"], KMM)])
            dve(lambda e: e.tensor_copy(kx(0)[0:64, :], kpk[0:64, 0:T]),
                waits=[(sem["act"], KPKd)])
            dve(lambda e: e.tensor_copy(kx(1)[0:64, :], kpk[64:128, 0:T]))
            dve(lambda e: e.memset(kx(0)[64:65, :], 1.0))
            dve(lambda e: e.memset(kx(1)[64:65, :], 1.0))
            QKX = sem["dve"].n
            for sl in range(T // 512):
                pe(lambda e, sl=sl: e.matmul(
                    psA[0:2, sl * 512:(sl + 1) * 512], c_bd2[:],
                    stg[:, sl * 512:(sl + 1) * 512], start=True, stop=True),
                   waits=[(sem["act"], SQQ)] if sl == 0 else [],
                   inc=(sl == T // 512 - 1))
            QSTAT = sem["pe"].n
            SQK = act(lambda e: e.activation(stg2[:, 0:T], kpk[:, 0:T], AF.Square))
            qn2r = scr[0:2, 0:T]
            siir = scr[0:2, 4096:4096 + T]
            tmpr = scr[0:2, 6144:6144 + T]
            act(lambda e: e.activation(tmpr, psA[0:2, 0:T], AF.Ln,
                                       bias=epsc[0:2, 1:2]),
                waits=[(sem["pe"], QSTAT)])
            QAB = act(lambda e: e.activation(qn2r, tmpr, AF.Exp, scale=0.5))
            SIIP = dve(lambda e: e.tensor_tensor(stg[:, 0:T], qpk[:, 0:T],
                                                 kpk[:, 0:T], ALU.mult),
                       waits=[(sem["pe"], QSTAT)])
            VSPL = min(6, TT)
            for tt in range(VSPL):
                for dc in range(DC):
                    pe(lambda e, tt=tt, dc=dc: e.matmul(
                        psB[:, tt * 128:(tt + 1) * 128],
                        bigB[:, dc * T + tt * 128:dc * T + (tt + 1) * 128],
                        wbv(dc), start=(dc == 0), stop=(dc == DC - 1)),
                       waits=[(sem["act"], KPKd)] if tt == 0 and dc == 0 else [],
                       inc=(dc == DC - 1))
            for sl in range(T // 512):
                pe(lambda e, sl=sl: e.matmul(
                    psA[32:34, sl * 512:(sl + 1) * 512], c_bd2[:],
                    stg2[:, sl * 512:(sl + 1) * 512], start=True, stop=True),
                   waits=[(sem["act"], SQK)] if sl == 0 else [],
                   inc=(sl == T // 512 - 1))
            KSTAT = sem["pe"].n
            dve(lambda e: e.reduce_max(smc[0:2, 0:1], psA[32:34, 0:T], axis=AX.X),
                waits=[(sem["pe"], KSTAT)])
            act(lambda e: e.activation(smc[0:2, 1:2], smc[0:2, 0:1], AF.Ln,
                                       bias=epsc[0:2, 1:2]), waits=[W("dve")])
            SKM = act(lambda e: e.activation(smc[0:2, 2:3], smc[0:2, 1:2], AF.Exp,
                                             scale=0.5))
            dve(lambda e: e.tensor_scalar(smc[0:2, 3:4], smc[0:2, 2:3], -1.0, None,
                                          ALU.mult), waits=[(sem["act"], SKM)])
            dve(lambda e, g=g: e.tensor_scalar(smc[0:2, 5:6], mbcol[g][:], -1.0,
                                               30.0, ALU.mult, ALU.add),
                waits=[(sem["d_g"], MBCC)])
            SMCD = dve(lambda e, g=g: e.tensor_scalar(smc[0:2, 6:7], ccol[g][:],
                                                      -1.0, None, ALU.mult))
            TMC = act(lambda e: e.activation(tmpr, qn2r, AF.Copy,
                                             scale=smc[0:2, 3:4]),
                      waits=[(sem["dve"], SMCD)])
            dve(lambda e: e.tensor_scalar(tmpr, tmpr, smc[0:2, 5:6], None,
                                          ALU.add), waits=[(sem["act"], TMC)])
            for sl in range(T // 512):
                pe(lambda e, sl=sl: e.matmul(
                    psA[64:66, sl * 512:(sl + 1) * 512], c_bd2[:],
                    stg[:, sl * 512:(sl + 1) * 512], start=True, stop=True),
                   waits=[(sem["dve"], SIIP)] if sl == 0 else [],
                   inc=(sl == T // 512 - 1))
            SII = sem["pe"].n
            SIIA = act(lambda e: e.activation(siir, psA[64:66, 0:T], AF.Copy,
                                              scale=-1.0),
                       waits=[(sem["pe"], SII)])
            NSH = dve(lambda e: e.scalar_tensor_tensor(
                siir, siir, smc[0:2, 6:7], tmpr, ALU.add, ALU.min),
                waits=[(sem["act"], SIIA)])
            SH1 = dmaq("gpsimd", "d_s1", lambda e: e.dma_start(
                out=qx(0)[64:65, :], in_=siir[0:1, :]), waits=[(sem["dve"], NSH)])
            SH2 = dmaq("gpsimd", "d_s2", lambda e: e.dma_start(
                out=qx(1)[64:65, :], in_=siir[1:2, :]))
            for tt in range(VSPL, TT):
                for dc in range(DC):
                    pe(lambda e, tt=tt, dc=dc: e.matmul(
                        psB[:, tt * 128:(tt + 1) * 128],
                        bigB[:, dc * T + tt * 128:dc * T + (tt + 1) * 128],
                        wbv(dc), start=(dc == 0), stop=(dc == DC - 1)),
                       inc=(dc == DC - 1))
            VMM = sem["pe"].n
            prev_vmm = VMM
            if g == NG - 1:
                for dcw in range(5, 8):
                    dmaq("gpsimd", "d_wo", lambda e, dcw=dcw: e.dma_start(
                        out=wb[:, (dcw - 5) * 1024:(dcw - 4) * 1024],
                        in_=woT[dcw * 128:(dcw + 1) * 128, :]),
                        waits=[(sem["pe"], VMM)] if dcw == 5 else [])
                WOLD = sem["d_wo"].n
            dve(lambda e: e.memset(bigA[:, VX0:VX0 + 2 * TT * 65], 1.0),
                waits=[(sem["pe"], VMM)])
            for tt in range(TT):
                act(lambda e, tt=tt: e.activation(
                    vxA(tt)[:, 0:64], psB[:, tt * 128:tt * 128 + 64], AF.Copy),
                    waits=[W("dve")] if tt == 0 else [])
                act(lambda e, tt=tt: e.activation(
                    vxB(tt)[:, 0:64], psB[:, tt * 128 + 64:(tt + 1) * 128], AF.Copy))

            hl0_fl = None
            for hl in range(2):
                hh = h0 + hl
                qxv = qx(hl)
                kxv = kx(hl)
                vxv = vxA if hl == 0 else vxB
                SHW = SH1 if hl == 0 else SH2
                shs = "d_s1" if hl == 0 else "d_s2"
                avs = hl * 512
                dinv_f = scr[0:1, 0:T]
                dtmp = scr[0:1, 2048:2048 + T]
                expw = {}
                drain_act = {}
                casts = [None] * NSUP

                def emit_s(isup, jj, hl=hl, qxv=qxv, kxv=kxv, SHW=SHW, shs=shs,
                           expw=expw, hh=hh):
                    slot = jj % 4
                    jmax = 4 * isup + 3
                    offc = max(0, jj * 128 - isup * 512)
                    has_diag = (isup * 4 <= jj <= jmax)
                    has_off = (isup * 4 <= jj + 1 <= jmax)
                    w = []
                    if jj == 0 and isup == 0:
                        w += [(sem[shs], SHW), (sem["dve"], QKX)]
                        if last_expall is not None:
                            w.append((sem["act"], last_expall))
                    if jj >= 4:
                        w.append((sem["act"], expw[(isup, jj - 4)]))
                    elif isup > 0:
                        w.append((sem["act"],
                                  expw[(isup - 1, 4 * (isup - 1) + slot)]))
                    pe(lambda e, jj=jj, slot=slot, offc=offc, isup=isup,
                       hd=has_diag, ho=has_off, qxv=qxv, kxv=kxv: e.matmul(
                        psA[:, slot * 512 + offc:slot * 512 + 512],
                        kxv[:, jj * 128:(jj + 1) * 128],
                        qxv[:, isup * 512 + offc:(isup + 1) * 512],
                        start=True, stop=not (hd or ho)),
                       waits=w, inc=not (has_diag or has_off))
                    if has_diag:
                        co = jj * 128 - isup * 512
                        pe(lambda e, slot=slot, co=co, hh=hh, ho=has_off: e.matmul(
                            psA[:, slot * 512 + co:slot * 512 + co + 128],
                            c_id[:], biasd_s[:, hh * 128:(hh + 1) * 128],
                            start=False, stop=not ho),
                           inc=not has_off)
                    if has_off:
                        co = (jj + 1) * 128 - isup * 512
                        pe(lambda e, slot=slot, co=co, hh=hh: e.matmul(
                            psA[:, slot * 512 + co:slot * 512 + co + 128],
                            c_id[:], biaso_s[:, hh * 128:(hh + 1) * 128],
                            start=False, stop=True),
                           inc=True)
                    MMD = sem["pe"].n
                    expw[(isup, jj)] = act(
                        lambda e, jj=jj, slot=slot, offc=offc: e.activation(
                            attnT(jj)[:, offc:512],
                            psA[:, slot * 512 + offc:slot * 512 + 512], AF.Exp),
                        waits=[(sem["pe"], MMD)])

                def emit_av(isup, jj, vxv=vxv, avs=avs, expw=expw,
                            drain_act=drain_act):
                    jmax = 4 * isup + 3
                    w = [(sem["act"], expw[(isup, jj)])]
                    if jj == 0:
                        if isup > 0:
                            w.append((sem["act"], drain_act[isup - 1]))
                        elif prev_mults is not None and avs // 512 in prev_mults:
                            w.append((sem["dve"], prev_mults[avs // 512]))
                    offc = max(0, jj * 128 - isup * 512)
                    pe(lambda e, jj=jj, avs=avs, vxv=vxv, jmax=jmax, offc=offc:
                       e.matmul(
                        psB[0:65, avs + offc:avs + 512], vxv(jj),
                        attnT(jj)[:, offc:512],
                        start=(jj == 0), stop=(jj == jmax)),
                       waits=w, inc=(jj == jmax))

                for isup in range(NSUP):
                    jmax = 4 * isup + 3
                    emit_s(isup, 0)
                    if hl == 1 and isup == 0:
                        p1 = [s for s in range(NSL - 1) if s != hl0_fl["sl_own"]]
                        if hl0_fl["sl_own"] < NSL - 1:
                            p1.append(hl0_fl["sl_own"])
                        flush_part(hl0_fl, p1)
                    for jj in range(1, jmax + 1):
                        emit_s(isup, jj)
                        if hl == 1 and isup == 0 and jj == 1:
                            nxt = avs // 512
                            mo = sorted(range(NSL),
                                        key=lambda s: 0 if s == nxt else 1 + s)
                            flush_part(hl0_fl, [NSL - 1], mult_order=mo)
                            prev_mults = hl0_fl["mults"]
                        if jj >= 2:
                            emit_av(isup, jj - 2)
                    emit_av(isup, jmax - 1)
                    emit_av(isup, jmax)
                    AVD = sem["pe"].n
                    act(lambda e, hh=hh, hl=hl, isup=isup, avs=avs: e.activation(
                        ynT[hl * 64:hl * 64 + 64,
                            (hh // 2) * T + isup * 512:
                            (hh // 2) * T + (isup + 1) * 512],
                        psB[0:64, avs:avs + 512], AF.Copy),
                        waits=[(sem["pe"], AVD)])
                    act(lambda e, isup=isup, avs=avs, dtmp=dtmp: e.activation(
                        dtmp[0:1, isup * 512:(isup + 1) * 512],
                        psB[64:65, avs:avs + 512], AF.Copy),
                        waits=([(sem["dve"], last_casts[isup])]
                               if last_casts is not None else []))
                    drain_act[isup] = sem["act"].n
                    dve(lambda e, isup=isup, dinv_f=dinv_f, dtmp=dtmp:
                        e.reciprocal(
                            dinv_f[0:1, isup * 512:(isup + 1) * 512],
                            dtmp[0:1, isup * 512:(isup + 1) * 512]),
                        waits=[(sem["act"], drain_act[isup])])
                    casts[isup] = dve(
                        lambda e, isup=isup, hl=hl, dinv_f=dinv_f: e.tensor_copy(
                            sqb[32 * hl:32 * hl + 1,
                                isup * 512:(isup + 1) * 512],
                            dinv_f[0:1, isup * 512:(isup + 1) * 512]))
                last_expall = expw[(NSUP - 1, 4 * (NSUP - 1) + 3)]
                last_casts = casts
                fl = {"hl": hl, "hg": g, "sl_own": avs // 512, "cast": casts,
                      "drain3": drain_act[NSUP - 1], "bc_pe": {}, "mults": {},
                      "h0drain3": hl0_fl["drain3"] if hl == 1 else None,
                      "mlast": None}
                if hl == 0:
                    hl0_fl = fl
                elif g == NG - 1:
                    p1 = [s for s in range(NSL - 1) if s != fl["sl_own"]]
                    if fl["sl_own"] < NSL - 1:
                        p1.append(fl["sl_own"])
                    flush_part(fl, p1)
                    mo = sorted(range(NSL), key=lambda s: 0 if s == 0 else 1 + s)
                    flush_part(fl, [NSL - 1], mult_order=mo)
                    prev_mults = fl["mults"]
                    prev_mlast = fl["mlast"]
                else:
                    prev_flush = fl
        ATT_DVE = sem["dve"].n
        ATT_PE = sem["pe"].n
        ATT_ACT = sem["act"].n
        if dbg:
            dbg_xn = dint("dbg_xn", [128, 8 * T], BF16)
            dbg_yn = dint("dbg_yn", [128, 8 * T], BF16)
            dbg_q = dint("dbg_q", [65, 2 * T], BF16)
            dbg_k = dint("dbg_k", [65, 2 * T], BF16)
            dmaq("sync", "d_c", lambda e: e.dma_start(out=dbg_xn[:], in_=bigB[:, 0:8 * T]),
                 waits=[(sem["dve"], ATT_DVE), (sem["act"], ATT_ACT), (sem["pe"], ATT_PE)])
            dmaq("sync", "d_c", lambda e: e.dma_start(out=dbg_yn[:], in_=ynT[:, 0:8 * T]))
            dmaq("sync", "d_c", lambda e: e.dma_start(out=dbg_q[:], in_=bigA[0:65, 0:2 * T]))
            dmaq("sync", "d_c", lambda e: e.dma_start(out=dbg_k[:], in_=bigA[0:65, 2 * T:4 * T]))

        wo_col = lambda dc: (3072 + dc * 1024) if dc < 5 else (dc - 5) * 1024
        xldc = [scr[:, 4096 + i * 1024:4096 + (i + 1) * 1024] for i in range(2)]
        stile = [scr[:, i * 1024:(i + 1) * 1024] for i in range(2)]
        x1f32 = scr[:, 2048:3072]
        x1st = [sqb[:, 0:1024], sqb[:, 1024:2048]]
        dxst = [bigA[:, QPK0 + 2 * T + i * 1024:QPK0 + 2 * T + (i + 1) * 1024]
                for i in range(2)] if QPK0 + 2 * T + 2048 <= WA else \
               [bigA[:, 0:1024], bigA[:, 1024:2048]]
        s_done = []
        ag_done = []
        dx_done = []
        for tt in range(TT):
            slot = tt % 2
            xs = "d_x0" if slot == 0 else "d_x1"
            ags = "d_ta" if slot == 0 else "d_tb"
            dxs = "d_oa" if slot == 0 else "d_ob"
            ps = psA if slot == 0 else psB
            w = [(sem["dve"], s_done[tt - 2])] if tt >= 2 else             [(sem["dve"], ATT_DVE), (sem["act"], ATT_ACT),
                 (sem["d_s1"], sem["d_s1"].n), (sem["d_s2"], sem["d_s2"].n)]
            XLD = dmaq("sync", xs, lambda e, tt=tt, slot=slot: e.dma_start(
                out=xldc[slot][:], in_=xtm[tt * 128:(tt + 1) * 128, :]), waits=w)
            w0 = [(sem["d_wo"], WOLD), (sem["dve"], ATT_DVE), (sem["act"], ATT_ACT)] \
                if tt == 0 else []
            if tt >= 2:
                w0 = w0 + [(sem["dve"], s_done[tt - 2])]
            for dc in range(DC):
                for sl in range(2):
                    pe(lambda e, tt=tt, dc=dc, ps=ps, sl=sl: e.matmul(
                        ps[:, sl * 512:(sl + 1) * 512],
                        ynT[:, dc * T + tt * 128:dc * T + (tt + 1) * 128],
                        wb[:, wo_col(dc) + sl * 512:wo_col(dc) + (sl + 1) * 512],
                        start=(dc == 0), stop=(dc == DC - 1)),
                       waits=w0 if dc == 0 and sl == 0 else [],
                       inc=(dc == DC - 1 and sl == 1))
            OMM = sem["pe"].n
            sw = [(sem["pe"], OMM), (sem[xs], XLD)]
            if tt >= 2:
                sw.append((sem[ags], ag_done[tt - 2]))
                sw.append((sem[dxs], dx_done[tt - 2]))
            s_done.append(dve(lambda e, slot=slot, ps=ps: e.scalar_tensor_tensor(
                stile[slot][:], xldc[slot][:], 2.0, ps[:, 0:1024], ALU.mult, ALU.add,
                accum_out=smc[:, 0:1]), waits=sw))
            SQ2 = act(lambda e, slot=slot: e.activation(
                x1st[slot][:], stile[slot][:], AF.Square, accum_out=smc[:, 1:2]),
                waits=[W("dve")])
            dve(lambda e: e.tensor_scalar(smc[:, 0:1], smc[:, 0:1], 1.0 / D, None, ALU.mult),
                waits=[(sem["act"], SQ2)])
            dve(lambda e: e.tensor_scalar(smc[:, 1:2], smc[:, 1:2], 1.0 / D, None, ALU.mult))
            dve(lambda e: e.tensor_tensor(smc[:, 3:4], smc[:, 0:1], smc[:, 0:1], ALU.mult))
            VAR2 = dve(lambda e: e.tensor_tensor(smc[:, 1:2], smc[:, 1:2], smc[:, 3:4],
                                                 ALU.subtract))
            act(lambda e: e.activation(smc[:, 4:5], smc[:, 1:2], AF.Ln, bias=epsc[:, 0:1]),
                waits=[(sem["dve"], VAR2)])
            RS2 = act(lambda e: e.activation(smc[:, 2:3], smc[:, 4:5], AF.Exp, scale=-0.5))
            dve(lambda e, slot=slot: e.tensor_scalar(
                x1f32[:], stile[slot][:], smc[:, 0:1], smc[:, 2:3],
                ALU.subtract, ALU.mult), waits=[(sem["act"], RS2)])
            X1BF = act(lambda e, slot=slot: e.activation(x1st[slot][:], x1f32[:], AF.Copy),
                       waits=[W("dve")])
            DXD = dve(lambda e, slot=slot: e.tensor_tensor(
                dxst[slot][:], x1f32[:], x1st[slot][:], ALU.subtract),
                waits=[(sem["act"], X1BF)])
            ag_done.append(dmaq("sync", ags, lambda e, tt=tt, slot=slot: e.dma_start(
                out=ag_in[tt * 128:(tt + 1) * 128, :], in_=x1st[slot][:]),
                waits=[(sem["act"], X1BF)]))
            dx_done.append(dmaq("sync", dxs, lambda e, tt=tt, slot=slot: e.dma_start(
                out=dx_dram[tt * 128:(tt + 1) * 128, :], in_=dxst[slot][:]),
                waits=[(sem["dve"], DXD)]))
        X1AG = sem["d_ta"].n
        X1BG = sem["d_tb"].n
        X1OA = sem["d_oa"].n
        X1OB = sem["d_ob"].n

    if do_coll:
        emit("gpsimd", lambda e: e.collective_compute(
            "AllGather", ALU.bypass, replica_groups=[list(range(NCORES))],
            ins=[ag_in[:]], outs=[x1_full[:]]),
            inc=sem["coll"], inck=16,
            waits=[(sem["d_ta"], X1AG), (sem["d_tb"], X1BG),
                   (sem["d_oa"], X1OA), (sem["d_ob"], X1OB)])
        AGX1 = sem["coll"].n
    if do_a:

        lgmm_done = []
        for dc in range(DC):
            trs = "d_ta" if dc % 2 == 0 else "d_tb"
            wtr = [(sem["d_ta"], X1AG), (sem["d_tb"], X1BG),
                   (sem["d_oa"], X1OA), (sem["d_ob"], X1OB)] if dc < 2 else               [(sem["pe"], lgmm_done[dc - 2])]
            ab0 = (dc % 2) * 2 * T
            dmaq("sync", trs, lambda e, dc=dc, ab0=ab0: e.dma_start(
                out=atb[:, ab0:ab0 + T], in_=ag_in[:, dc * 128:(dc + 1) * 128],
                transpose=True), waits=wtr)
            TRD = dmaq("sync", trs, lambda e, dc=dc, ab0=ab0: e.dma_start(
                out=atb[:, ab0 + T:ab0 + 2 * T], in_=dx_dram[:, dc * 128:(dc + 1) * 128],
                transpose=True))
            for tt in range(TT):
                pe(lambda e, dc=dc, tt=tt, ab0=ab0: e.matmul(
                    psA[:, tt * 8:(tt + 1) * 8], atb[:, ab0 + tt * 128:ab0 + (tt + 1) * 128],
                    rw_s[:, dc * E:(dc + 1) * E], start=True, stop=False),
                   waits=([(sem[trs], TRD), (sem["d_rw"], RWLD)] +
                          ([W("dve")] if dc >= 1 else [])) if tt == 0 else [],
                   inc=False)
                pe(lambda e, dc=dc, tt=tt, ab0=ab0: e.matmul(
                    psA[:, tt * 8:(tt + 1) * 8],
                    atb[:, ab0 + T + tt * 128:ab0 + T + (tt + 1) * 128],
                    rw_s[:, dc * E:(dc + 1) * E], start=False, stop=True),
                   inc=(tt == TT - 1))
            lgmm_done.append(sem["pe"].n)
            if dc == 0:
                dve(lambda e: e.tensor_copy(scr[:, 1024:1024 + TT * 8],
                                            psA[:, 0:TT * 8]),
                    waits=[(sem["pe"], lgmm_done[-1])])
            else:
                dve(lambda e: e.tensor_tensor(scr[:, 1024:1024 + TT * 8],
                                              scr[:, 1024:1024 + TT * 8],
                                              psA[:, 0:TT * 8], ALU.add),
                    waits=[(sem["pe"], lgmm_done[-1])])
        LGMM = sem["pe"].n
        dve(lambda e: e.tensor_tensor(scr[:, 1024:1024 + TT * 8],
                                      scr[:, 1024:1024 + TT * 8],
                                      rbb_s[:].rearrange("p e -> p one e", one=1)
                                      .broadcast(1, TT) if False else rbb_s[:],
                                      ALU.add)) if False else None
        for tt in range(TT):
            dve(lambda e, tt=tt: e.tensor_tensor(
                scr[:, 1024 + tt * 8:1024 + (tt + 1) * 8],
                scr[:, 1024 + tt * 8:1024 + (tt + 1) * 8], rbb_s[:], ALU.add))
        LGE = sem["dve"].n
        LGST = dmaq("sync", "d_t", lambda e: e.dma_start(
            out=lg_in.rearrange("(c p) e -> p c e", p=128),
            in_=scr[:, 1024:1024 + TT * 8].rearrange("p (c e) -> p c e", e=E)),
            waits=[(sem["dve"], LGE)])
    if do_coll:
        emit("gpsimd", lambda e: e.collective_compute(
            "AllGather", ALU.bypass, replica_groups=[list(range(NCORES))],
            ins=[lg_in[:]], outs=[lg_full[:]]),
            inc=sem["coll"], inck=16, waits=[(sem["d_t"], LGST)])
        AGLG = sem["coll"].n
    if do_b:
        SE_ = S * E
        YN_PE = sem["pe"].n
        YN_DVE = sem["dve"].n
        wsl = [wb[:, i * 1024:(i + 1) * 1024] for i in range(2)]
        vsl = [wb[:, 2048 + i * 1024:2048 + (i + 1) * 1024] for i in range(2)]
        yst = [wb[:, 4096 + i * 1024:4096 + (i + 1) * 1024] for i in range(2)]
        WOL = dmaq("gpsimd", "d_w", lambda e: e.dma_start(
            out=ynT[:, 0:HC * 1024].rearrange("p (c o) -> p c o", o=1024),
            in_=WoutE.rearrange("(c p) o -> p c o", p=128)),
            waits=[(sem["pe"], YN_PE), (sem["dve"], YN_DVE)])
        pre_wvl = {}
        for hc in range(2):
            wvs = "d_wa" if hc == 0 else "d_wb"
            dmaq("gpsimd", wvs, lambda e, hc=hc: e.dma_start(
                out=wsl[hc][:].rearrange("p (c h) -> p c h", h=128),
                in_=We[:, hc * 128:(hc + 1) * 128].rearrange("(c p) h -> p c h", p=128)))
            pre_wvl[hc] = dmaq("gpsimd", wvs, lambda e, hc=hc: e.dma_start(
                out=vsl[hc][:].rearrange("p (c h) -> p c h", h=128),
                in_=Ve[:, hc * 128:(hc + 1) * 128].rearrange("(c p) h -> p c h", p=128)))
        lgt = scr[:, 0:SE_]
        cumf = scr[:, SE_:2 * SE_]
        lg2 = scr[:, 2 * SE_:3 * SE_]
        base = 3 * SE_

        def rvv(i):
            return scr[:, base + i * S:base + (i + 1) * S]
        v1, v2, id1, id2, w1, w2, t1, t2, t3, t4 = (rvv(i) for i in range(10))

        LGF = dmaq("sync", "d_r", lambda e: e.dma_start(
            out=lgt.rearrange("p (s e) -> p s e", e=E),
            in_=lg_full.rearrange("(s p) e -> p s e", p=128)),
            waits=[(sem["coll"], AGLG)])
        lgt3 = lgt.rearrange("p (s e) -> p s e", e=E)
        lgte = lambda ee: lgt.rearrange("p (s e) -> p e s", e=E)[:, ee, :]
        lg2e = lambda ee: lg2.rearrange("p (s e) -> p e s", e=E)[:, ee, :]
        dve(lambda e: e.reduce_max(v1, lgt3, axis=AX.X),
            waits=[(sem["d_r"], LGF), W("d_c")])
        dve(lambda e: e.memset(id1, 1e9))
        for ee in range(E):
            dve(lambda e, ee=ee: e.tensor_tensor(t1, lgte(ee), v1, ALU.is_ge))
            dve(lambda e, ee=ee: e.tensor_scalar(t1, t1, -1e6, 1e6 + ee, ALU.mult, ALU.add))
            dve(lambda e: e.tensor_tensor(id1, id1, t1, ALU.min))
        dve(lambda e: e.tensor_copy(lg2, lgt))
        for ee in range(E):
            dve(lambda e, ee=ee: e.tensor_scalar(t1, id1, float(ee), None, ALU.is_equal))
            dve(lambda e: e.tensor_scalar(t1, t1, -1e30, None, ALU.mult))
            dve(lambda e, ee=ee: e.tensor_tensor(lg2e(ee), lg2e(ee), t1, ALU.add))
        lg23 = lg2.rearrange("p (s e) -> p s e", e=E)
        dve(lambda e: e.reduce_max(v2, lg23, axis=AX.X))
        dve(lambda e: e.memset(id2, 1e9))
        for ee in range(E):
            dve(lambda e, ee=ee: e.tensor_tensor(t1, lg2e(ee), v2, ALU.is_ge))
            dve(lambda e, ee=ee: e.tensor_scalar(t1, t1, -1e6, 1e6 + ee, ALU.mult, ALU.add))
            dve(lambda e: e.tensor_tensor(id2, id2, t1, ALU.min))
        dve(lambda e: e.tensor_tensor(t1, v1, v2, ALU.subtract))
        SGW = act(lambda e: e.activation(w1, t1, AF.Sigmoid), waits=[W("dve")])
        dve(lambda e: e.tensor_scalar(w2, w1, -1.0, 1.0, ALU.mult, ALU.add),
            waits=[(sem["act"], SGW)])
        for ee in range(E):
            dve(lambda e, ee=ee: e.tensor_scalar(t1, id1, float(ee), None, ALU.is_equal))
            dve(lambda e, ee=ee: e.tensor_scalar(t2, id2, float(ee), None, ALU.is_equal))
            dve(lambda e, ee=ee: e.tensor_tensor(
                sqb[:, ee * S:(ee + 1) * S], t1, t2, ALU.add))
        AIND = sem["dve"].n
        for ee in range(E):
            pe(lambda e, ee=ee: e.matmul(
                psA[:, ee * S:(ee + 1) * S], c_uti[:], sqb[:, ee * S:(ee + 1) * S],
                start=True, stop=True),
               waits=[(sem["dve"], AIND), W("act"), W("d_c")] if ee == 0 else [],
               inc=False)
            pe(lambda e, ee=ee: e.matmul(
                psB[0:1, ee * S:(ee + 1) * S], c_o128[:], sqb[:, ee * S:(ee + 1) * S],
                start=True, stop=True),
               inc=(ee == E - 1))
        CPFX = sem["pe"].n
        trow = lg2[0:1, 0:SE_]
        dve(lambda e: e.tensor_copy(trow, psB[0:1, 0:SE_]), waits=[(sem["pe"], CPFX)])
        t3r = lambda ap: ap.rearrange("one (e s) -> one e s", s=S)
        rowA = trow
        rowB = lgt[0:1, 0:SE_]
        d = 1
        cur, oth = rowA, rowB
        while d < S:
            dve(lambda e, cur=cur, oth=oth, d=d: e.tensor_copy(
                t3r(oth)[:, :, 0:d], t3r(cur)[:, :, 0:d]))
            dve(lambda e, cur=cur, oth=oth, d=d: e.tensor_tensor(
                t3r(oth)[:, :, d:S], t3r(cur)[:, :, d:S], t3r(cur)[:, :, 0:S - d], ALU.add))
            cur, oth = oth, cur
            d *= 2
        dve(lambda e, oth=oth: e.tensor_copy(oth, psB[0:1, 0:SE_]))
        EXCR = dve(lambda e, cur=cur, oth=oth: e.tensor_tensor(cur, cur, oth, ALU.subtract))
        for ee in range(E):
            pe(lambda e, ee=ee, cur=cur: e.matmul(
                psB[:, ee * S:(ee + 1) * S], c_o1f[:], cur[0:1, ee * S:(ee + 1) * S],
                start=True, stop=True),
               waits=[(sem["dve"], EXCR)] if ee == 0 else [],
               inc=(ee == E - 1))
        CBC = sem["pe"].n
        dve(lambda e: e.tensor_copy(cumf, psB[:, 0:SE_]), waits=[(sem["pe"], CBC)])
        dve(lambda e: e.tensor_tensor(cumf, cumf, psA[:, 0:SE_], ALU.add))
        cum3 = lambda ee: cumf.rearrange("p (e s) -> p e s", s=S)[:, ee, :]
        dve(lambda e: e.memset(t3, 0.0))
        dve(lambda e: e.memset(t4, 0.0))
        for ee in range(E):
            dve(lambda e, ee=ee: e.tensor_scalar(t1, id1, float(ee), None, ALU.is_equal))
            dve(lambda e, ee=ee: e.tensor_tensor(t1, t1, cum3(ee), ALU.mult))
            dve(lambda e: e.tensor_tensor(t3, t3, t1, ALU.add))
            dve(lambda e, ee=ee: e.tensor_scalar(t2, id2, float(ee), None, ALU.is_equal))
            dve(lambda e, ee=ee: e.tensor_tensor(t2, t2, cum3(ee), ALU.mult))
            dve(lambda e: e.tensor_tensor(t4, t4, t2, ALU.add))
        dve(lambda e: e.tensor_scalar(v1, t3, CAP + 0.5, None, ALU.is_le))
        dve(lambda e: e.tensor_scalar(v2, t4, CAP + 0.5, None, ALU.is_le))

        def build_idx(idsl, sel, kk, dsti):
            dve(lambda e, idsl=idsl: e.tensor_scalar(t1, idsl, float(CAP), None, ALU.mult))
            dve(lambda e, sel=sel: e.tensor_tensor(t1, t1, sel, ALU.add))
            dve(lambda e: e.tensor_scalar(t1, t1, -1.0, None, ALU.add))
            dve(lambda e, kk=kk: e.tensor_tensor(t1, t1, kk, ALU.mult))
            return dve(lambda e, dsti=dsti: e.tensor_copy(dsti[:], t1))
        IX1D = build_idx(id1, t3, v1, ix1_s)
        IX2D = build_idx(id2, t4, v2, ix2_s)
        dmaq("sync", "d_ks", lambda e: e.dma_start(out=idx1_dram[:], in_=ix1_s[:]),
             waits=[(sem["dve"], IX1D)])
        dmaq("sync", "d_ks", lambda e: e.dma_start(out=idx2_dram[:], in_=ix2_s[:]),
             waits=[(sem["dve"], IX2D)])
        dmaq("sync", "d_ks", lambda e: e.dma_start(out=k1_dram[:], in_=v1),
             waits=[(sem["dve"], IX2D)])
        KSALL = dmaq("sync", "d_ks", lambda e: e.dma_start(out=k2_dram[:], in_=v2))
        emit("gpsimd", lambda e: e.dma_start(
            out=kst[:, 0:SP], in_=k1_dram[:, bass.ts(e.partition_id(), SP)]),
            inc=sem["d_ix"], inck=16, waits=[(sem["d_ks"], KSALL)])
        emit("gpsimd", lambda e: e.dma_start(
            out=kst[:, SP:2 * SP], in_=k2_dram[:, bass.ts(e.partition_id(), SP)]),
            inc=sem["d_ix"], inck=16)
        emit("gpsimd", lambda e: e.dma_start(
            out=ixst[:, 0:SP], in_=idx1_dram[:, bass.ts(e.partition_id(), SP)]),
            inc=sem["d_ix"], inck=16)
        emit("gpsimd", lambda e: e.dma_start(
            out=ixst[:, SP:2 * SP], in_=idx2_dram[:, bass.ts(e.partition_id(), SP)]),
            inc=sem["d_ix"], inck=16)
        IXALL = sem["d_ix"].n
        if part == "b":
            dmaq("sync", "d_t", lambda e: e.dma_start(out=ixst_out[:], in_=ixst[:, 0:2 * SP]),
                 waits=[(sem["d_ix"], IXALL)])
            dmaq("sync", "d_t", lambda e: e.dma_start(out=kst_out[:], in_=kst[:, 0:2 * SP]))

        dve(lambda e: e.memset(t4, 0.0))
        for ee in range(E):
            dve(lambda e, ee=ee: e.tensor_scalar(smc[:, 4:5], eid_s[:], float(ee), None,
                                                 ALU.is_equal))
            dve(lambda e, ee=ee: e.tensor_scalar(t3, cum3(ee), smc[:, 4:5], None, ALU.mult))
            dve(lambda e: e.tensor_tensor(t4, t4, t3, ALU.add))
        dve(lambda e: e.tensor_scalar(t1, id1, eid_s[:, 0:1], None, ALU.is_equal))
        dve(lambda e: e.tensor_scalar(t2, id2, eid_s[:, 0:1], None, ALU.is_equal))
        dve(lambda e: e.tensor_tensor(t1, t1, t2, ALU.add))
        dve(lambda e: e.tensor_scalar(t3, t4, CAP + 0.5, None, ALU.is_le))
        dve(lambda e: e.tensor_tensor(t1, t1, t3, ALU.mult))
        dve(lambda e: e.tensor_scalar(t3, t4, -1.0, None, ALU.add))
        dve(lambda e: e.tensor_tensor(t3, t3, t1, ALU.mult))
        dve(lambda e: e.tensor_scalar(t2, t1, -1.0, 1.0, ALU.mult, ALU.add))
        dve(lambda e: e.tensor_scalar(t4, c_cid2[:], float(CAP), None, ALU.add),
            waits=[W("d_c")])
        dve(lambda e: e.tensor_tensor(t2, t2, t4, ALU.mult))
        dve(lambda e: e.tensor_tensor(t3, t3, t2, ALU.add))
        OFF1 = dve(lambda e: e.tensor_copy(off_s[:], t3))
        dve(lambda e: e.tensor_copy(tw_s[:], c_tw[:]), waits=[W("d_c")])
        dve(lambda e: e.tensor_scalar(t2, id1, eid_s[:, 0:1], None, ALU.is_equal))
        dve(lambda e: e.tensor_tensor(t2, t2, w1, ALU.mult))
        dve(lambda e: e.tensor_scalar(t4, id2, eid_s[:, 0:1], None, ALU.is_equal))
        dve(lambda e: e.tensor_tensor(t4, t4, w2, ALU.mult))
        dve(lambda e: e.tensor_tensor(t2, t2, t4, ALU.add))
        dve(lambda e: e.tensor_tensor(t2, t2, t1, ALU.mult))
        WSEL = dve(lambda e: e.tensor_copy(
            tw_s[:, 0:2 * S].rearrange("p (s two) -> p s two", two=2)[:, :, 1:2],
            t2.rearrange("p (s one) -> p s one", one=1)))
        GWI = dmaq("sync", "d_gw", lambda e: e.dma_start(
            out=gw_dram[:, 0:2].rearrange("(c p) two -> p c two", p=128),
            in_=c_gwi[:, 0:2 * (CT + 1)].rearrange("p (c two) -> p c two", two=2)),
            waits=[W("d_c")])
        for sc in range(S):
            dmaq("gpsimd", "d_sc", lambda e, sc=sc: e.indirect_dma_start(
                gw_dram[:], IndirectOffsetOnAxis(off_s[:, sc:sc + 1], 0),
                tw_s[:, 2 * sc:2 * sc + 128], None),
                waits=[(sem["d_gw"], GWI), (sem["dve"], max(OFF1, WSEL))] if sc == 0 else [])
        GWSC = sem["d_sc"].n
    if do_b:
        xinP = bigA[:, 0:CT * 1024]
        MSX = dve(lambda e: e.memset(xinP, 0.0), waits=[(sem["pe"], sem["pe"].n), W("d_c")])
        gwq = ["sync", "scalar", "sync", "scalar"]
        CQ = max(CT // 4, 1)
        for kq in range(CT // CQ):
            dmaq(gwq[kq % 4], "d_gw", lambda e, kq=kq: e.dma_start(
                out=gwt_s[:, 2 * kq * CQ:2 * (kq + 1) * CQ].rearrange(
                    "p (c two) -> p c two", two=2),
                in_=gw_dram[kq * CQ * 128:(kq + 1) * CQ * 128, 0:2].rearrange(
                    "(c p) two -> p c two", p=128)),
                waits=[(sem["d_sc"], GWSC)])
        GWL = sem["d_gw"].n
        for jb in range(CT):
            dve(lambda e, jb=jb: e.tensor_copy(gcol_i[:, jb:jb + 1],
                                               gwt_s[:, 2 * jb:2 * jb + 1]),
                waits=[(sem["d_gw"], GWL)] if jb == 0 else [])
        GCI = sem["dve"].n
        xinT = lambda dc: bigB[:, dc * CAP:(dc + 1) * CAP]
        gaw = []
        tpd = {}
        xdr = {}
        gq = lambda jb: "d_gc" if jb % 2 == 0 else "d_gd"

        def emit_gather(jb):
            w = []
            if jb == 0:
                w = [(sem["dve"], GCI), (sem["dve"], MSX), (sem["coll"], AGX1)]
            if jb >= 2:
                w = w + [(sem["pe"], tpd[jb - 2])]
            gaw.append(dmaq("gpsimd", gq(jb), lambda e, jb=jb: e.indirect_dma_start(
                xinP[:, jb * 1024:(jb + 1) * 1024], None,
                x1_full[:], IndirectOffsetOnAxis(gcol_i[:, jb:jb + 1], 0)),
                waits=w))

        emit_gather(0)
        emit_gather(1)
        for jb in range(CT):
            pb = (jb % 2) * 1024
            for dcc in range(DC):
                w = []
                if dcc == 0:
                    w = [(sem[gq(jb)], gaw[jb])]
                    if jb == 0:
                        w.append(W("dve"))
                    if jb >= 2:
                        w.append((sem["act"], xdr[jb - 2]))
                pe(lambda e, jb=jb, dcc=dcc, pb=pb: e.matmul(
                    psB[:, pb + dcc * 128:pb + (dcc + 1) * 128],
                    xinP[:, jb * 1024 + dcc * 128:jb * 1024 + (dcc + 1) * 128],
                    c_id[:], start=True, stop=True),
                   waits=w, inc=(dcc == DC - 1))
            tpd[jb] = sem["pe"].n
            if jb + 2 < CT:
                emit_gather(jb + 2)
            for dcc in range(DC):
                act(lambda e, jb=jb, dcc=dcc, pb=pb: e.activation(
                    bigB[:, dcc * CAP + jb * 128:dcc * CAP + (jb + 1) * 128],
                    psB[:, pb + dcc * 128:pb + (dcc + 1) * 128], AF.Copy),
                    waits=[(sem["pe"], tpd[jb])] if dcc == 0 else [])
            xdr[jb] = sem["act"].n
        XTRA = sem["act"].n
        woutc = lambda hc: ynT[:, hc * 1024:(hc + 1) * 1024]
        hrg = lambda hc: bigA[:, hc * TB:(hc + 1) * TB]
        ab_done = {}
        ys_done = []
        for tb in range(NTB):
            for hc in range(HC):
                slot = hc % 2
                w = [W("pe")] if (tb == 0 and hc < 2) else []
                prev = (tb, hc - 2) if hc >= 2 else (tb - 1, HC - 2 + hc)
                if prev in ab_done:
                    w = [(sem["pe"], ab_done[prev])]
                wvs = "d_wa" if slot == 0 else "d_wb"
                if tb == 0 and hc < 2:
                    WVL = pre_wvl[hc]
                else:
                    dmaq("gpsimd", wvs, lambda e, hc=hc, slot=slot: e.dma_start(
                        out=wsl[slot][:].rearrange("p (c h) -> p c h", h=128),
                        in_=We[:, hc * 128:(hc + 1) * 128].rearrange("(c p) h -> p c h", p=128)),
                        waits=w)
                    WVL = dmaq("gpsimd", wvs, lambda e, hc=hc, slot=slot: e.dma_start(
                        out=vsl[slot][:].rearrange("p (c h) -> p c h", h=128),
                        in_=Ve[:, hc * 128:(hc + 1) * 128].rearrange("(c p) h -> p c h", p=128)))
                NSL = TB // 512
                for dc in range(DC):
                    for sl in range(NSL):
                        pe(lambda e, dc=dc, slot=slot, tb=tb, sl=sl: e.matmul(
                            psA[:, sl * 512:(sl + 1) * 512],
                            wsl[slot][:, dc * 128:(dc + 1) * 128],
                            xinT(dc)[:, tb * TB + sl * 512:tb * TB + (sl + 1) * 512],
                            start=(dc == 0), stop=(dc == DC - 1)),
                           waits=[(sem[wvs], WVL), (sem["act"], XTRA), W("dve")]
                           if dc == 0 and sl == 0 else [], inc=False)
                for dc in range(DC):
                    for sl in range(NSL):
                        pe(lambda e, dc=dc, slot=slot, tb=tb, sl=sl: e.matmul(
                            psA[:, TB + sl * 512:TB + (sl + 1) * 512],
                            vsl[slot][:, dc * 128:(dc + 1) * 128],
                            xinT(dc)[:, tb * TB + sl * 512:tb * TB + (sl + 1) * 512],
                            start=(dc == 0), stop=(dc == DC - 1)),
                           inc=(dc == DC - 1 and sl == NSL - 1))
                ABMM = sem["pe"].n
                ab_done[(tb, hc)] = ABMM
                SIL = act(lambda e: e.activation(sqb[:, 0:TB], psA[:, TB:2 * TB], AF.Silu),
                          waits=[(sem["pe"], ABMM)])
                dve(lambda e, hc=hc: e.tensor_tensor(hrg(hc), psA[:, 0:TB], sqb[:, 0:TB],
                                                     ALU.mult),
                    waits=[(sem["act"], SIL)])
            HD = sem["dve"].n
            for ttl in range(TB // 128):
                gt = tb * (TB // 128) + ttl
                slot = ttl % 2
                po = slot * 1024
                w0 = [(sem["dve"], HD), (sem["d_w"], WOL)]
                if len(ys_done) >= 2:
                    w0.append((sem["dve"], ys_done[-2][0]))
                for hc in range(HC):
                    for sl in range(2):
                        pe(lambda e, hc=hc, ttl=ttl, po=po, sl=sl: e.matmul(
                            psB[:, po + sl * 512:po + (sl + 1) * 512],
                            bigA[:, hc * TB + ttl * 128:hc * TB + (ttl + 1) * 128],
                            woutc(hc)[:, sl * 512:(sl + 1) * 512],
                            start=(hc == 0), stop=(hc == HC - 1)),
                           waits=w0 if hc == 0 and sl == 0 else [],
                           inc=(hc == HC - 1 and sl == 1))
                YMM = sem["pe"].n
                oss = "d_oa" if slot == 0 else "d_ob"
                ysw = [(sem["pe"], YMM), (sem["d_gw"], GWL)]
                if len(ys_done) >= 2:
                    ysw.append((sem[oss], ys_done[-2][1]))
                YS = dve(lambda e, gt=gt, slot=slot, po=po: e.tensor_scalar(
                    yst[slot][:], psB[:, po:po + 1024],
                    gwt_s[:, 2 * gt + 1:2 * gt + 2], None, ALU.mult),
                    waits=ysw)
                YSD = dmaq("sync", oss, lambda e, gt=gt, slot=slot: e.dma_start(
                    out=Y_in[gt * 128:(gt + 1) * 128, :], in_=yst[slot][:]),
                    waits=[(sem["dve"], YS)])
                ys_done.append((YS, YSD))
        YSTA = sem["d_oa"].n
        YSTB = sem["d_ob"].n
    if do_coll:
        emit("gpsimd", lambda e: e.collective_compute(
            "AllGather", ALU.bypass, replica_groups=[list(range(NCORES))],
            ins=[Y_in[:]], outs=[Y_full[:]]),
            inc=sem["coll"], inck=16,
            waits=[(sem["d_oa"], YSTA), (sem["d_ob"], YSTB)])
        AGY = sem["coll"].n
    if part == "c":
        dmaq("sync", "d_ix", lambda e: e.dma_start(out=ixst[:, 0:2 * SP], in_=ixst_in[:]))
        dmaq("sync", "d_ix", lambda e: e.dma_start(out=kst[:, 0:2 * SP], in_=kst_in[:]))
        IXALL = sem["d_ix"].n
    if do_c:
        y1g = lambda tt: bigA[:, tt * 1024:(tt + 1) * 1024]
        y2g = lambda tt: bigB[:, tt * 1024:(tt + 1) * 1024]
        dve(lambda e: e.memset(bigA[:, 0:TT * 1024], 0.0),
            waits=[(sem["pe"], sem["pe"].n), W("d_c")])
        MSY = dve(lambda e: e.memset(bigB[:, 0:TT * 1024], 0.0))
        for tt in range(TT):
            dmaq("gpsimd", "d_gd", lambda e, tt=tt: e.indirect_dma_start(
                y1g(tt), None, Y_full[:], IndirectOffsetOnAxis(ixst[:, tt:tt + 1], 0)),
                waits=[(sem["dve"], MSY), (sem["coll"], AGY), (sem["d_ix"], IXALL)]
                if tt == 0 else [])
            dmaq("gpsimd", "d_gd", lambda e, tt=tt: e.indirect_dma_start(
                y2g(tt), None, Y_full[:],
                IndirectOffsetOnAxis(ixst[:, SP + tt:SP + tt + 1], 0)))
        YGA = sem["d_gd"].n
        x1ld = [ynT[:, i * 1024:(i + 1) * 1024] for i in range(2)]
        dxld = [ynT[:, 2048 + i * 1024:2048 + (i + 1) * 1024] for i in range(2)]
        ch_done = []
        st_done = []
        PEFIN = sem["pe"].n
        for tt in range(TT):
            slot = tt % 2
            xs = "d_x0" if slot == 0 else "d_x1"
            oss = "d_oa" if slot == 0 else "d_ob"
            w = [(sem["dve"], ch_done[tt - 2])] if tt >= 2 else [(sem["pe"], PEFIN)]
            dmaq("sync", xs, lambda e, tt=tt, slot=slot: e.dma_start(
                out=x1ld[slot][:], in_=ag_in[tt * 128:(tt + 1) * 128, :]), waits=w)
            XL2 = dmaq("sync", xs, lambda e, tt=tt, slot=slot: e.dma_start(
                out=dxld[slot][:], in_=dx_dram[tt * 128:(tt + 1) * 128, :]))
            s2 = scr[:, 0:1024]
            ot = scr[:, 1024:2048]
            wv = [(sem[xs], XL2), (sem["d_gd"], YGA)]
            if tt == 0:
                wv.append(W("d_c"))
            if tt >= 1:
                wv.append((sem["d_oa" if (tt - 1) % 2 == 0 else "d_ob"], st_done[tt - 1]))
            dve(lambda e, slot=slot: e.tensor_tensor(s2, x1ld[slot][:], dxld[slot][:],
                                                     ALU.add),
                waits=wv + ([(sem["d_ks"], KSALL)] if tt == 0 else []))
            dve(lambda e, tt=tt: e.scalar_tensor_tensor(
                s2, y1g(tt), kst[:, tt:tt + 1], s2, ALU.mult, ALU.add))
            dve(lambda e, tt=tt: e.scalar_tensor_tensor(
                s2, y2g(tt), kst[:, SP + tt:SP + tt + 1], s2, ALU.mult, ALU.add))
            dve(lambda e: e.reduce_sum(smc[:, 0:1], s2, axis=AX.X))
            MUC = dve(lambda e: e.tensor_scalar(smc[:, 0:1], smc[:, 0:1], 1.0 / D,
                                                None, ALU.mult))
            act(lambda e: e.activation(sqb[:, 0:1024], s2, AF.Square,
                                       accum_out=smc[:, 1:2]),
                waits=[(sem["dve"], MUC)])
            SQD = sem["act"].n
            dve(lambda e: e.tensor_scalar(smc[:, 1:2], smc[:, 1:2], 1.0 / D, None,
                                          ALU.mult), waits=[(sem["act"], SQD)])
            dve(lambda e: e.tensor_tensor(smc[:, 3:4], smc[:, 0:1], smc[:, 0:1], ALU.mult))
            VARG = dve(lambda e: e.tensor_tensor(smc[:, 1:2], smc[:, 1:2], smc[:, 3:4],
                                                 ALU.subtract))
            act(lambda e: e.activation(smc[:, 4:5], smc[:, 1:2], AF.Ln,
                                       bias=epsc[:, 0:1]), waits=[(sem["dve"], VARG)])
            RSG = act(lambda e: e.activation(smc[:, 2:3], smc[:, 4:5], AF.Exp, scale=-0.5))
            OD = dve(lambda e, ot=ot: e.tensor_scalar(
                ot, s2, smc[:, 0:1], smc[:, 2:3], ALU.subtract, ALU.mult),
                waits=[(sem["act"], RSG)])
            ch_done.append(OD)
            st_done.append(dmaq("sync", oss, lambda e, tt=tt, ot=ot: e.dma_start(
                out=out[tt * 128:(tt + 1) * 128, :], in_=ot),
                waits=[(sem["dve"], OD)]))

    waitonly("sync", [(sem[n], sem[n].n) for n in sem])

    with nc.Block() as block:
        block.sync(lambda e: [f(e) for f in sched["sync"]])
        block.tensor(lambda e: [f(e) for f in sched["tensor"]])
        block.vector(lambda e: [f(e) for f in sched["vector"]])
        block.scalar(lambda e: [f(e) for f in sched["scalar"]])
        block.gpsimd(lambda e: [f(e) for f in sched["gpsimd"]])
    nc.finalize()
    return nc


def make_inmaps(inputs, T=2048):
    x = np.ascontiguousarray(np.asarray(inputs["L0_attn_x"], np.float32))
    Wq = np.asarray(inputs["L0_attn_Wq"], np.float32)
    Wk = np.asarray(inputs["L0_attn_Wk"], np.float32)
    Wv = np.asarray(inputs["L0_attn_Wv"], np.float32)
    Wo = np.asarray(inputs["L0_attn_Wo"], np.float32)
    rel = np.asarray(inputs["L0_attn_rel"], np.float32)
    gam = np.asarray(inputs["L0_attn_ln_gamma"], np.float32)
    bet = np.asarray(inputs["L0_attn_ln_beta"], np.float32)
    rw = np.asarray(inputs["L0_ffn_router_w"], np.float32)
    rb = np.asarray(inputs["L0_ffn_router_b"], np.float32)
    Wf = np.asarray(inputs["L0_ffn_W"], np.float32)
    Vf = np.asarray(inputs["L0_ffn_V"], np.float32)
    Wof = np.asarray(inputs["L0_ffn_W_out"], np.float32)

    NGq = H // 2
    wqTg = np.ascontiguousarray(Wq.T.reshape(D, NGq, 128).transpose(1, 0, 2))
    wkTg = np.ascontiguousarray(Wk.T.reshape(D, NGq, 128).transpose(1, 0, 2))
    wvTg = np.ascontiguousarray(Wv.T.reshape(D, NGq, 128).transpose(1, 0, 2))
    woT = np.ascontiguousarray(Wo.T)
    gammac = np.ascontiguousarray(gam.reshape(DC, 128).T)
    betac = np.ascontiguousarray(bet.reshape(DC, 128).T)
    il = np.arange(128)
    dd = np.clip(il[None, :] - il[:, None], 0, 128)
    do = np.clip(128 + il[None, :] - il[:, None], 0, 128)
    biasdT = rel[:, dd].copy()
    biasdT[:, il[None, :] < il[:, None]] = -1e30
    biasoT = rel[:, do].copy()
    rel128b = np.tile(rel[:, 128][None, :], (128, 1))
    rwT = np.ascontiguousarray(rw.T)
    rbb = np.tile(rb[None, :], (128, 1))

    in_maps = []
    for c in range(NCORES):
        in_maps.append(dict(
            xT=np.ascontiguousarray(x[c].T), xtm=x[c],
            wqTg=wqTg, wkTg=wkTg, wvTg=wvTg, woT=woT,
            gammac=gammac, betac=betac,
            biasdT=np.ascontiguousarray(biasdT),
            biasoT=np.ascontiguousarray(biasoT),
            relin=rel, rel128b=np.ascontiguousarray(rel128b),
            rwT=rwT, rbb=np.ascontiguousarray(rbb),
            We=np.ascontiguousarray(Wf[c]), Ve=np.ascontiguousarray(Vf[c]),
            WoutE=np.ascontiguousarray(Wof[c]),
            eidcol=np.full((128, 1), float(c), np.float32),
        ))
    return in_maps


_NC_CACHE = {}


def _nc(part, T=2048):
    key = (part, T)
    if key not in _NC_CACHE:
        _NC_CACHE[key] = build_nc(T=T, CAP=2048, part=part)
    return _NC_CACHE[key]


def kernel(**inputs):
    from concourse.bass_utils import run_bass_kernel_spmd
    T = 2048
    base = make_inmaps(inputs, T=T)
    cores = list(range(NCORES))

    res = run_bass_kernel_spmd(_nc("a", T), base, core_ids=cores)
    ag = [np.asarray(r["ag_in"]) for r in res.results]
    dx = [np.asarray(r["dx_dram"]) for r in res.results]
    lg = [np.asarray(r["lg_in"]) for r in res.results]
    x1_full = np.concatenate(ag, 0)
    lg_full = np.concatenate(lg, 0)

    res = run_bass_kernel_spmd(
        _nc("b", T),
        [dict(m, x1_full=x1_full, lg_full=lg_full) for m in base],
        core_ids=cores)
    Y_full = np.concatenate([np.asarray(r["Y_in"]) for r in res.results], 0)
    ixst = [np.asarray(r["ixst_out"]) for r in res.results]
    kst = [np.asarray(r["kst_out"]) for r in res.results]

    res = run_bass_kernel_spmd(
        _nc("c", T),
        [dict(base[c], Y_full=Y_full, ixst_in=ixst[c], kst_in=kst[c],
              ag_in=ag[c], dx_dram=dx[c]) for c in range(NCORES)],
        core_ids=cores)
    return np.stack([np.asarray(r["out"], np.float32) for r in res.results], 0)


# version 13 (final)
# speedup vs baseline: 2.0171x; 1.0133x over previous
import numpy as np
import ml_dtypes
import concourse.bass as bass
import concourse.mybir as mybir
from concourse.bass_core_rust import IndirectOffsetOnAxis
from concourse import dve_ops

F32 = mybir.dt.float32
BF16 = mybir.dt.bfloat16
I32 = mybir.dt.int32
AF = mybir.ActivationFunctionType
ALU = mybir.AluOpType
AX = mybir.AxisListType
BF = ml_dtypes.bfloat16

NCORES = 8
D = 1024
DC = 8
H = 16
E = 8
HID = 2048
HC = 16
OOB = 131072
EPS = 1e-5


class Sem:
    def __init__(self, nc, name):
        self.h = nc.alloc_semaphore(name)
        self.n = 0


def build_nc(T=2048, CAP=2048, dbg=False, part="all"):
    TT = T // 128
    N = NCORES * T
    S = N // 128
    CT = CAP // 128
    SP = T // 128
    NSUP = T // 512
    TB = min(1024, CAP)
    NTB = CAP // TB
    NG = H // 2

    nc = bass.Bass("TRN2", target_bir_lowering=False, debug=False,
                   dynamic_dma_scratch_size=32768)

    def din(name, shape, dt=F32):
        return nc.dram_tensor(name, list(shape), dt, kind="ExternalInput").ap()

    xT = din("xT", [D, T])
    xtm = din("xtm", [T, D])
    wqTg = din("wqTg", [NG, D, 128])
    wkTg = din("wkTg", [NG, D, 128])
    wvTg = din("wvTg", [NG, D, 128])
    woT = din("woT", [D, D])
    gammac = din("gammac", [128, DC])
    betac = din("betac", [128, DC])
    biasdT = din("biasdT", [H, 128, 128])
    biasoT = din("biasoT", [H, 128, 128])
    relin = din("relin", [H, 129])
    rel128b = din("rel128b", [128, H])
    rwT = din("rwT", [D, E])
    rbb = din("rbb", [128, E])
    We = din("We", [D, HID])
    Ve = din("Ve", [D, HID])
    WoutE = din("WoutE", [HID, D])
    eidcol = din("eidcol", [128, 1])
    out = nc.dram_tensor("out", [T, D], F32, kind="ExternalOutput").ap()

    def dint(name, shape, dt):
        return nc.dram_tensor(name, list(shape), dt, kind="Internal").ap()

    if part == "a":
        ag_in = nc.dram_tensor("ag_in", [T, D], BF16, kind="ExternalOutput").ap()
        dx_dram = nc.dram_tensor("dx_dram", [T, D], BF16, kind="ExternalOutput").ap()
        lg_in = nc.dram_tensor("lg_in", [T, E], F32, kind="ExternalOutput").ap()
        x1_full = None
        lg_full = None
    elif part == "b":
        x1_full = din("x1_full", [N, D], BF16)
        lg_full = din("lg_full", [N, E], F32)
        ag_in = None
        dx_dram = None
        lg_in = None
    elif part == "c":
        ag_in = din("ag_in", [T, D], BF16)
        dx_dram = din("dx_dram", [T, D], BF16)
        x1_full = None
        lg_full = None
        lg_in = None
    else:
        ag_in = dint("ag_in", [T, D], BF16)
        x1_full = dint("x1_full", [N, D], BF16)
        dx_dram = dint("dx_dram", [T, D], BF16)
        lg_in = dint("lg_in", [T, E], F32)
        lg_full = dint("lg_full", [N, E], F32)
    gw_dram = dint("gw_dram", [CAP + 128, 128], F32)
    k1_dram = dint("k1_dram", [128, S], F32)
    k2_dram = dint("k2_dram", [128, S], F32)
    idx1_dram = dint("idx1_dram", [128, S], I32)
    idx2_dram = dint("idx2_dram", [128, S], I32)
    xin_dram = dint("xin_dram", [CAP, D], BF16)
    if part == "b":
        Y_in = nc.dram_tensor("Y_in", [CAP, D], BF16, kind="ExternalOutput").ap()
        Y_full = None
        ixst_out = nc.dram_tensor("ixst_out", [128, 2 * (T // 128)], I32, kind="ExternalOutput").ap()
        kst_out = nc.dram_tensor("kst_out", [128, 2 * (T // 128)], F32, kind="ExternalOutput").ap()
    elif part == "c":
        Y_in = None
        Y_full = din("Y_full", [E * CAP, D], BF16)
        ixst_in = din("ixst_in", [128, 2 * (T // 128)], I32)
        kst_in = din("kst_in", [128, 2 * (T // 128)], F32)
    else:
        Y_in = dint("Y_in", [CAP, D], BF16)
        Y_full = dint("Y_full", [E * CAP, D], BF16)

    ident_bf = nc.inline_tensor(np.eye(128, dtype=BF), "c_ident").ap()
    ones128_bf = nc.inline_tensor(np.ones((128, 1), dtype=BF), "c_o128").ap()
    ones1_bf = nc.inline_tensor(np.ones((1, 128), dtype=BF), "c_o1").ap()
    ones1_f32 = nc.inline_tensor(np.ones((1, 128), dtype=np.float32), "c_o1f").ap()
    ultri_bf = nc.inline_tensor(
        np.triu(np.ones((128, 128), np.float32), 1).astype(BF), "c_ultri").ap()
    bd2_np = np.zeros((128, 2), dtype=BF)
    bd2_np[0:64, 0] = 1
    bd2_np[64:128, 1] = 1
    bd2_bf = nc.inline_tensor(bd2_np, "c_bd2").ap()
    utri_incl_bf = nc.inline_tensor(
        np.triu(np.ones((128, 128), np.float32), 0).astype(BF), "c_utinc").ap()
    tw_np = np.zeros((128, 2 * S + 128), np.float32)
    tw_np[:, 0:2 * S:2] = (np.arange(S)[None, :] * 128 + np.arange(128)[:, None])
    tw_c = nc.inline_tensor(tw_np, "c_tw").ap()
    colid_c = nc.inline_tensor(
        np.tile(np.arange(S, dtype=np.float32)[None, :], (128, 1)), "c_colid").ap()
    gwi_np = np.zeros((128, 2 * (CT + 1)), np.float32)
    gwi_c = nc.inline_tensor(gwi_np, "c_gwi").ap()
    epsc_np = np.zeros((128, 2), np.float32)
    epsc_np[:, 0] = EPS
    epsc_np[:, 1] = 1e-30
    eps_c = nc.inline_tensor(epsc_np, "c_eps").ap()

    def sb(name, shape, dt):
        return nc.alloc_sbuf_tensor(name, list(shape), dt)

    c_id = sb("c_id", [128, 128], BF16)
    c_o128 = sb("c_o128s", [128, 1], BF16)
    c_o1 = sb("c_o1s", [1, 128], BF16)
    c_o1f = sb("c_o1fs", [1, 128], F32)
    c_ut = sb("c_uts", [128, 128], BF16)
    c_bd2 = sb("c_bd2s", [128, 2], BF16)
    c_uti = sb("c_utis", [128, 128], BF16)
    c_tw = sb("c_tws", [128, 2 * S + 128], F32)
    c_gwi = sb("c_gwis", [128, 2 * (CT + 1)], F32)
    c_cid2 = sb("c_cid2", [128, S], F32)
    epsc = sb("epsc", [128, 2], F32)
    gam_s = sb("gam_s", [128, DC], F32)
    bet_s = sb("bet_s", [128, DC], F32)
    rbb_s = sb("rbb_s", [128, E], F32)
    eid_s = sb("eid_s", [128, 1], F32)
    rel_s = sb("rel_s", [H, 129], F32)
    r128_s = sb("r128_s", [128, H], F32)
    relsm = sb("relsm", [H, 2], F32)
    mbcol = [sb(f"mbc{g}", [2, 1], F32) for g in range(NG)]
    ccol = [sb(f"ccc{g}", [2, 1], F32) for g in range(NG)]
    biasd_s = sb("biasd_s", [128, H * 128], BF16)
    biaso_s = sb("biaso_s", [128, H * 128], BF16)
    bstage = sb("bstage", [128, 128], F32)
    rw_s = sb("rw_s", [128, DC * E], BF16)
    smc = sb("smc", [128, 16], F32)
    ix1_s = sb("ix1_s", [128, S], I32)
    ix2_s = sb("ix2_s", [128, S], I32)
    off_s = sb("off_s", [128, S], I32)
    tw_s = sb("tw_s", [128, 2 * S + 128], F32)
    gwt_s = sb("gwt_s", [128, 2 * CT], F32)
    ixst = sb("ixst", [128, 32], I32)
    drec = sb("drec", [128, 32], F32)
    kst = sb("kst", [128, 32], F32)
    gcol_i = sb("gcol_i", [128, CT], I32)

    WA = max(8 * T + 256, HC * TB, SP * 1024)
    WB = max(8 * T, CT * 1024, 8 * CAP, SP * 1024)
    WY = max(8 * T, HC * 1024, 4 * 4096)
    bigA = sb("bigA", [128, WA], BF16)
    bigB = sb("bigB", [128, WB], BF16)
    scr = sb("scr", [128, 8192], F32)
    wb = sb("wb", [128, 8 * 1024], BF16)
    atb = sb("atb", [128, max(16 * 512, 4 * T)], BF16)
    ynT = sb("ynT", [128, WY], BF16)
    sqb = sb("sqb", [128, 2048], BF16)

    psA = nc.alloc_psum_tensor("psA", [128, 2048], F32)
    psB = nc.alloc_psum_tensor("psB", [128, 2048], F32)

    xbf = lambda dc: bigA[:, dc * T:(dc + 1) * T]
    xnbf = lambda dc: bigB[:, dc * T:(dc + 1) * T]
    qx = lambda hl: bigA[0:65, hl * T:(hl + 1) * T]
    kx = lambda hl: bigA[0:65, (2 + hl) * T:(3 + hl) * T]
    VX0 = 4 * T
    vxA = lambda tt: bigA[:, VX0 + tt * 65:VX0 + (tt + 1) * 65]
    VX1 = 4 * T + TT * 65
    vxB = lambda tt: bigA[:, VX1 + tt * 65:VX1 + (tt + 1) * 65]
    QPK0 = 4 * T + 2 * TT * 65
    qpk = bigA[:, QPK0:QPK0 + T]
    kpk = bigA[:, QPK0 + T:QPK0 + 2 * T]
    stgw = min(T, 2048)
    stg = bigA[:, QPK0 + 2 * T:QPK0 + 2 * T + stgw]
    stg2 = sb("stg2", [128, 2048], BF16)
    assert QPK0 + 2 * T + stgw <= WA
    attnT = lambda jj: atb[:, jj * 512:(jj + 1) * 512]

    ENGS = ("sync", "tensor", "vector", "scalar", "gpsimd")
    sched = {e: [] for e in ENGS}
    sem = {n: Sem(nc, "s_" + n) for n in
           ("dve", "act", "pe", "d_c", "d_x0", "d_x1", "d_w", "d_wa", "d_wb",
            "d_g", "d_rw", "d_t", "d_ta", "d_tb", "d_oa", "d_ob",
            "d_r", "d_ks", "d_gw", "d_sc", "d_gc", "d_gd", "d_ix",
            "d_s1", "d_s2", "d_rc", "d_wo", "coll")}

    def emit(eng, build_fn, inc=None, inck=1, waits=()):
        wcap = [(s.h, v) for (s, v) in waits if v > 0]
        tgt = None
        if inc is not None:
            inc.n += inck
            tgt = inc.h

        def closure(e, bf=build_fn, wcap=wcap, tgt=tgt, k=inck):
            for (hh, vv) in wcap:
                e.wait_ge(hh, vv)
            ins = bf(e)
            if tgt is not None:
                ins.then_inc(tgt, k)
        sched[eng].append(closure)

    def waitonly(eng, waits):
        wcap = [(s.h, v) for (s, v) in waits if v > 0]

        def closure(e, wcap=wcap):
            for (hh, vv) in wcap:
                e.wait_ge(hh, vv)
        sched[eng].append(closure)

    def dve(build_fn, waits=()):
        emit("vector", build_fn, inc=sem["dve"],
             waits=((sem["dve"], sem["dve"].n),) + tuple(waits))
        return sem["dve"].n

    def act(build_fn, waits=()):
        emit("scalar", build_fn, inc=sem["act"],
             waits=((sem["act"], sem["act"].n),) + tuple(waits))
        return sem["act"].n

    def pe(build_fn, waits=(), inc=True):
        emit("tensor", build_fn, inc=sem["pe"] if inc else None, waits=waits)
        return sem["pe"].n

    def dmaq(eng, s, build_fn, waits=()):
        emit(eng, build_fn, inc=sem[s], inck=16, waits=waits)
        return sem[s].n

    W = lambda s: (sem[s], sem[s].n)

    _bccache = {}

    def bc(e, v):
        if v not in _bccache:
            _bccache[v] = e.snap(v)
        return _bccache[v]

    for (dst, src) in [(c_id, ident_bf), (c_o128, ones128_bf), (c_o1, ones1_bf), (c_o1f, ones1_f32),
                       (c_ut, ultri_bf), (c_bd2, bd2_bf),                        (c_uti, utri_incl_bf), (c_tw, tw_c), (c_gwi, gwi_c), (c_cid2, colid_c), (epsc, eps_c),
                       (gam_s, gammac), (bet_s, betac), (rbb_s, rbb),
                       (eid_s, eidcol), (rel_s, relin), (r128_s, rel128b)]:
        dmaq("sync", "d_c", lambda e, d=dst, s2=src: e.dma_start(out=d[:], in_=s2[:]))
    dmaq("gpsimd", "d_rw", lambda e: e.dma_start(
        out=rw_s[:].rearrange("p (c e) -> p c e", e=E),
        in_=rwT.rearrange("(c p) e -> p c e", p=128)))
    RWLD = sem["d_rw"].n

    do_a = part in ("all", "a")
    do_b = part in ("all", "b")
    do_c = part in ("all", "c")
    do_coll = part == "all"
    AGX1 = AGLG = AGY = 0
    KSALL = 0
    IXALL = 0
    if do_a:
        dve(lambda e: e.reduce_max(relsm[:, 0:1], rel_s[:], axis=AX.X), waits=[W("d_c")])
        dve(lambda e: e.tensor_tensor(relsm[:, 0:1], relsm[:, 0:1], rel_s[:, 128:129],
                                      ALU.subtract))
        RELSM = dve(lambda e: e.tensor_tensor(relsm[:, 1:2], rel_s[:, 0:1],
                                              rel_s[:, 128:129], ALU.subtract))
        for g in range(NG):
            dmaq("gpsimd", "d_g", lambda e, g=g: e.dma_start(
                out=mbcol[g][:], in_=relsm[2 * g:2 * g + 2, 0:1]),
                waits=[(sem["dve"], RELSM)])
            dmaq("gpsimd", "d_g", lambda e, g=g: e.dma_start(
                out=ccol[g][:], in_=relsm[2 * g:2 * g + 2, 1:2]))
        MBCC = sem["d_g"].n

        BDL = dmaq("sync", "d_rc", lambda e: e.dma_start(
            out=scr[:, 0:H * 128].rearrange("p (h i) -> p h i", i=128),
            in_=biasdT.rearrange("h p i -> p h i")))
        BOL = dmaq("gpsimd", "d_g", lambda e: e.dma_start(
            out=scr[:, 2048:2048 + H * 128].rearrange("p (h i) -> p h i", i=128),
            in_=biasoT.rearrange("h p i -> p h i")))
        MBCC = sem["d_g"].n
        for hh in range(H):
            dve(lambda e, hh=hh: e.tensor_scalar(
                biasd_s[:, hh * 128:(hh + 1) * 128], scr[:, hh * 128:(hh + 1) * 128],
                r128_s[:, hh:hh + 1], None, ALU.subtract),
                waits=[(sem["d_rc"], BDL)] if hh == 0 else [])
            dve(lambda e, hh=hh: e.tensor_scalar(
                biaso_s[:, hh * 128:(hh + 1) * 128],
                scr[:, 2048 + hh * 128:2048 + (hh + 1) * 128],
                r128_s[:, hh:hh + 1], None, ALU.subtract),
                waits=[(sem["d_g"], BOL)] if hh == 0 else [])

        xld = [scr[:, 4096 + i * 2048:4096 + i * 2048 + T] for i in range(2)]
        cast_done = []
        m2_done = []
        for dc in range(DC):
            slot = dc % 2
            xs = "d_x0" if slot == 0 else "d_x1"
            w = [(sem["dve"], cast_done[dc - 2])] if dc >= 2 else []
            XW = dmaq("sync", xs, lambda e, dc=dc, slot=slot: e.dma_start(
                out=xld[slot][:], in_=xT[dc * 128:(dc + 1) * 128, :]), waits=w)
            cast_done.append(dve(lambda e, dc=dc, slot=slot: e.tensor_copy(
                xbf(dc), xld[slot][:]), waits=[(sem[xs], XW)]))
            sqw = [W("dve")] + ([(sem["pe"], m2_done[-1])] if dc > 0 else [])
            SQA = act(lambda e, dc=dc: e.activation(sqb[:, 0:T], xbf(dc), AF.Square),
                      waits=sqw)
            for sl in range(T // 512):
                pe(lambda e, dc=dc, sl=sl: e.matmul(
                    psA[0:1, sl * 512:(sl + 1) * 512], c_o128[:],
                    bigA[:, dc * T + sl * 512:dc * T + (sl + 1) * 512],
                    start=(dc == 0), stop=(dc == DC - 1)),
                   waits=[(sem["dve"], cast_done[-1])] if sl == 0 else [], inc=False)
                m2_done.append(pe(lambda e, sl=sl, dc=dc: e.matmul(
                    psB[0:1, sl * 512:(sl + 1) * 512], c_o128[:],
                    sqb[:, sl * 512:(sl + 1) * 512],
                    start=(dc == 0), stop=(dc == DC - 1)),
                   waits=[(sem["act"], SQA)] if sl == 0 else [],
                   inc=(sl == T // 512 - 1)))
        PE_ST = sem["pe"].n

        rmu = scr[0:1, 4096:4096 + T]
        rv_ = scr[0:1, 6144:6144 + T]
        dve(lambda e: e.tensor_scalar(rmu, psA[0:1, 0:T], 1.0 / D, None, ALU.mult),
            waits=[(sem["pe"], PE_ST)])
        dve(lambda e: e.tensor_scalar(rv_, psB[0:1, 0:T], 1.0 / D, None, ALU.mult))
        dve(lambda e: e.tensor_tensor(psB[0:1, 0:T], rmu, rmu, ALU.mult))
        VARD = dve(lambda e: e.tensor_tensor(rv_, rv_, psB[0:1, 0:T], ALU.subtract))
        act(lambda e: e.activation(psB[0:1, 0:T], rv_, AF.Ln, bias=epsc[0:1, 0:1]), waits=[W("dve")])
        RSTA = act(lambda e: e.activation(rv_, psB[0:1, 0:T], AF.Exp, scale=-0.5))
        dve(lambda e: e.tensor_scalar(wb[0:1, 0:T], rmu, -1.0, None, ALU.mult),
            waits=[(sem["act"], RSTA)])
        ROWS_BF = dve(lambda e: e.tensor_copy(wb[0:1, T:2 * T], rv_))
        for sl in range(T // 512):
            pe(lambda e, sl=sl: e.matmul(
                psA[:, sl * 512:(sl + 1) * 512], c_o1[:],
                wb[0:1, sl * 512:(sl + 1) * 512], start=True, stop=True),
               waits=[(sem["dve"], ROWS_BF)] if sl == 0 else [], inc=False)
            pe(lambda e, sl=sl: e.matmul(
                psB[:, sl * 512:(sl + 1) * 512], c_o1[:],
                wb[0:1, T + sl * 512:T + (sl + 1) * 512], start=True, stop=True),
               inc=(sl == T // 512 - 1))
        PE_BC = sem["pe"].n
        act(lambda e: e.activation(scr[:, 4096:4096 + T], psA[:, 0:T], AF.Copy),
            waits=[(sem["pe"], PE_BC)])
        BCP = act(lambda e: e.activation(scr[:, 6144:6144 + T], psB[:, 0:T],
                                         AF.Copy))
        xn_done = []
        for dc in range(DC):
            dve(lambda e, dc=dc: e.tensor_tensor(scr[:, 0:T], xbf(dc),
                                                 scr[:, 4096:4096 + T], ALU.add),
                waits=[(sem["act"], BCP)] if dc == 0 else [])
            dve(lambda e: e.tensor_tensor(scr[:, 2048:2048 + T], scr[:, 0:T],
                                          scr[:, 6144:6144 + T], ALU.mult))
            xn_done.append(dve(lambda e, dc=dc: e.tensor_scalar(
                xnbf(dc), scr[:, 2048:2048 + T],
                gam_s[:, dc:dc + 1], bet_s[:, dc:dc + 1],
                ALU.mult, ALU.add)))
        XN_DONE = sem["dve"].n

        wbq = lambda dc: wb[:, dc * 128:(dc + 1) * 128]
        wbk = lambda dc: wb[:, 1024 + dc * 128:1024 + (dc + 1) * 128]
        wbv = lambda dc: wb[:, 2048 + dc * 128:2048 + (dc + 1) * 128]

        NSL = T // 512

        def flush_part(fl, sls, mult_order=None):
            for sl in sls:
                w = [(sem["dve"], fl["cast"][sl])]
                if sl == fl["sl_own"]:
                    w.append((sem["act"], fl["drain3"]))
                if fl["hl"] == 1 and sl == 0:
                    w.append((sem["act"], fl["h0drain3"]))
                fl["bc_pe"][sl] = pe(lambda e, sl=sl, hl=fl["hl"]: e.matmul(
                    psB[hl * 64:hl * 64 + 64, sl * 512:(sl + 1) * 512],
                    (c_o1[0:1, 0:64] if hl == 0 else c_uti[32:33, 64:128]),
                    sqb[32 * hl:32 * hl + 1, sl * 512:(sl + 1) * 512],
                    start=True, stop=True), waits=w, inc=True)
            if mult_order is not None:
                for sl in mult_order:
                    fl["mults"][sl] = dve(
                        lambda e, sl=sl, hl=fl["hl"], hg=fl["hg"]: e.tensor_tensor(
                            ynT[hl * 64:hl * 64 + 64,
                                hg * T + sl * 512:hg * T + (sl + 1) * 512],
                            ynT[hl * 64:hl * 64 + 64,
                                hg * T + sl * 512:hg * T + (sl + 1) * 512],
                            psB[hl * 64:hl * 64 + 64, sl * 512:(sl + 1) * 512],
                            ALU.mult),
                        waits=[(sem["pe"], fl["bc_pe"][sl])])
                fl["mlast"] = sem["dve"].n

        prev_vmm = None
        prev_flush = None
        prev_mults = None
        prev_mlast = None
        last_expall = None
        last_casts = None
        WOLD = None
        for g in range(NG):
            h0 = 2 * g
            for (wsrc, off) in ((wqTg, 0), (wkTg, 1024), (wvTg, 2048)):
                dmaq("gpsimd", "d_w", lambda e, wsrc=wsrc, off=off, g=g: e.dma_start(
                    out=wb[:, off:off + 1024].rearrange("p (c i) -> p c i", i=128),
                    in_=wsrc[g].rearrange("(c p) i -> p c i", p=128)),
                    waits=[(sem["pe"], prev_vmm if g > 0 else PE_BC)])
            WLD = sem["d_w"].n
            if g == 0:
                for dcw in range(5):
                    dmaq("gpsimd", "d_wo", lambda e, dcw=dcw: e.dma_start(
                        out=wb[:, 3072 + dcw * 1024:3072 + (dcw + 1) * 1024],
                        in_=woT[dcw * 128:(dcw + 1) * 128, :]),
                        waits=[(sem["pe"], PE_BC)] if dcw == 0 else [])
            for dc in range(DC):
                for sl in range(T // 512):
                    w = []
                    if dc == 0 and sl == 0:
                        w = [(sem["d_w"], WLD)]
                        if last_expall is not None:
                            w.append((sem["act"], last_expall))
                    if g == 0 and sl == 0:
                        w = w + [(sem["dve"], xn_done[dc])]
                        if dc == 0:
                            w.append((sem["act"], BCP))
                    pe(lambda e, dc=dc, sl=sl: e.matmul(
                        psA[:, sl * 512:(sl + 1) * 512], wbq(dc),
                        bigB[:, dc * T + sl * 512:dc * T + (sl + 1) * 512],
                        start=(dc == 0), stop=(dc == DC - 1)),
                       waits=w, inc=(dc == DC - 1 and sl == T // 512 - 1))
                if dc == 0 and prev_flush is not None:
                    p1 = [s for s in range(NSL - 1) if s != prev_flush["sl_own"]]
                    if prev_flush["sl_own"] < NSL - 1:
                        p1.append(prev_flush["sl_own"])
                    flush_part(prev_flush, p1)
                if dc == 1 and prev_flush is not None:
                    mo = sorted(range(NSL), key=lambda s: 0 if s == 0 else 1 + s)
                    flush_part(prev_flush, [NSL - 1], mult_order=mo)
                    prev_mults = prev_flush["mults"]
                    prev_mlast = prev_flush["mlast"]
                    prev_flush = None
            QMM = sem["pe"].n
            QPKd = act(lambda e: e.activation(qpk[:, 0:T], psA[:, 0:T], AF.Copy,
                                              scale=0.125),
                       waits=[(sem["pe"], QMM)])
            dve(lambda e: e.tensor_copy(qx(0)[0:64, :], qpk[0:64, 0:T]),
                waits=[(sem["act"], QPKd)])
            dve(lambda e: e.tensor_copy(qx(1)[0:64, :], qpk[64:128, 0:T]))
            SQQ = act(lambda e: e.activation(stg[:, 0:T], qpk[:, 0:T], AF.Square))
            for dc in range(DC):
                for sl in range(T // 512):
                    w = []
                    if dc == 0 and sl == 0:
                        w = [(sem["dve"], prev_mlast if prev_mlast is not None
                              else XN_DONE)]
                    pe(lambda e, dc=dc, sl=sl: e.matmul(
                        psB[:, sl * 512:(sl + 1) * 512], wbk(dc),
                        bigB[:, dc * T + sl * 512:dc * T + (sl + 1) * 512],
                        start=(dc == 0), stop=(dc == DC - 1)),
                       waits=w, inc=(dc == DC - 1 and sl == T // 512 - 1))
            KMM = sem["pe"].n
            KPKd = act(lambda e: e.activation(kpk[:, 0:T], psB[:, 0:T], AF.Copy),
                       waits=[(sem["pe"], KMM)])
            dve(lambda e: e.tensor_copy(kx(0)[0:64, :], kpk[0:64, 0:T]),
                waits=[(sem["act"], KPKd)])
            dve(lambda e: e.tensor_copy(kx(1)[0:64, :], kpk[64:128, 0:T]))
            dve(lambda e: e.memset(kx(0)[64:65, :], 1.0))
            dve(lambda e: e.memset(kx(1)[64:65, :], 1.0))
            QKX = sem["dve"].n
            for sl in range(T // 512):
                pe(lambda e, sl=sl: e.matmul(
                    psA[0:2, sl * 512:(sl + 1) * 512], c_bd2[:],
                    stg[:, sl * 512:(sl + 1) * 512], start=True, stop=True),
                   waits=[(sem["act"], SQQ)] if sl == 0 else [],
                   inc=(sl == T // 512 - 1))
            QSTAT = sem["pe"].n
            SQK = act(lambda e: e.activation(stg2[:, 0:T], kpk[:, 0:T], AF.Square))
            qn2r = scr[0:2, 0:T]
            siir = scr[0:2, 4096:4096 + T]
            tmpr = scr[0:2, 6144:6144 + T]
            act(lambda e: e.activation(tmpr, psA[0:2, 0:T], AF.Ln,
                                       bias=epsc[0:2, 1:2]),
                waits=[(sem["pe"], QSTAT)])
            QAB = act(lambda e: e.activation(qn2r, tmpr, AF.Exp, scale=0.5))
            SIIP = dve(lambda e: e.tensor_tensor(stg[:, 0:T], qpk[:, 0:T],
                                                 kpk[:, 0:T], ALU.mult),
                       waits=[(sem["pe"], QSTAT)])
            VSPL = min(6, TT)
            for tt in range(VSPL):
                for dc in range(DC):
                    pe(lambda e, tt=tt, dc=dc: e.matmul(
                        psB[:, tt * 128:(tt + 1) * 128],
                        bigB[:, dc * T + tt * 128:dc * T + (tt + 1) * 128],
                        wbv(dc), start=(dc == 0), stop=(dc == DC - 1)),
                       waits=[(sem["act"], KPKd)] if tt == 0 and dc == 0 else [],
                       inc=(dc == DC - 1))
            for sl in range(T // 512):
                pe(lambda e, sl=sl: e.matmul(
                    psA[32:34, sl * 512:(sl + 1) * 512], c_bd2[:],
                    stg2[:, sl * 512:(sl + 1) * 512], start=True, stop=True),
                   waits=[(sem["act"], SQK)] if sl == 0 else [],
                   inc=(sl == T // 512 - 1))
            KSTAT = sem["pe"].n
            dve(lambda e: e.reduce_max(smc[0:2, 0:1], psA[32:34, 0:T], axis=AX.X),
                waits=[(sem["pe"], KSTAT)])
            act(lambda e: e.activation(smc[0:2, 1:2], smc[0:2, 0:1], AF.Ln,
                                       bias=epsc[0:2, 1:2]), waits=[W("dve")])
            SKM = act(lambda e: e.activation(smc[0:2, 2:3], smc[0:2, 1:2], AF.Exp,
                                             scale=0.5))
            dve(lambda e: e.tensor_scalar(smc[0:2, 3:4], smc[0:2, 2:3], -1.0, None,
                                          ALU.mult), waits=[(sem["act"], SKM)])
            dve(lambda e, g=g: e.tensor_scalar(smc[0:2, 5:6], mbcol[g][:], -1.0,
                                               30.0, ALU.mult, ALU.add),
                waits=[(sem["d_g"], MBCC)])
            SMCD = dve(lambda e, g=g: e.tensor_scalar(smc[0:2, 6:7], ccol[g][:],
                                                      -1.0, None, ALU.mult))
            TMC = act(lambda e: e.activation(tmpr, qn2r, AF.Copy,
                                             scale=smc[0:2, 3:4]),
                      waits=[(sem["dve"], SMCD)])
            dve(lambda e: e.tensor_scalar(tmpr, tmpr, smc[0:2, 5:6], None,
                                          ALU.add), waits=[(sem["act"], TMC)])
            for sl in range(T // 512):
                pe(lambda e, sl=sl: e.matmul(
                    psA[64:66, sl * 512:(sl + 1) * 512], c_bd2[:],
                    stg[:, sl * 512:(sl + 1) * 512], start=True, stop=True),
                   waits=[(sem["dve"], SIIP)] if sl == 0 else [],
                   inc=(sl == T // 512 - 1))
            SII = sem["pe"].n
            SIIA = act(lambda e: e.activation(siir, psA[64:66, 0:T], AF.Copy,
                                              scale=-1.0),
                       waits=[(sem["pe"], SII)])
            NSH = dve(lambda e: e.scalar_tensor_tensor(
                siir, siir, smc[0:2, 6:7], tmpr, ALU.add, ALU.min),
                waits=[(sem["act"], SIIA)])
            SH1 = dmaq("gpsimd", "d_s1", lambda e: e.dma_start(
                out=qx(0)[64:65, :], in_=siir[0:1, :]), waits=[(sem["dve"], NSH)])
            SH2 = dmaq("gpsimd", "d_s2", lambda e: e.dma_start(
                out=qx(1)[64:65, :], in_=siir[1:2, :]))
            for tt in range(VSPL, TT):
                for dc in range(DC):
                    pe(lambda e, tt=tt, dc=dc: e.matmul(
                        psB[:, tt * 128:(tt + 1) * 128],
                        bigB[:, dc * T + tt * 128:dc * T + (tt + 1) * 128],
                        wbv(dc), start=(dc == 0), stop=(dc == DC - 1)),
                       inc=(dc == DC - 1))
            VMM = sem["pe"].n
            prev_vmm = VMM
            if g == NG - 1:
                for dcw in range(5, 8):
                    dmaq("gpsimd", "d_wo", lambda e, dcw=dcw: e.dma_start(
                        out=wb[:, (dcw - 5) * 1024:(dcw - 4) * 1024],
                        in_=woT[dcw * 128:(dcw + 1) * 128, :]),
                        waits=[(sem["pe"], VMM)] if dcw == 5 else [])
                WOLD = sem["d_wo"].n
            dve(lambda e: e.memset(bigA[:, VX0:VX0 + 2 * TT * 65], 1.0),
                waits=[(sem["pe"], VMM)])
            for tt in range(TT):
                act(lambda e, tt=tt: e.activation(
                    vxA(tt)[:, 0:64], psB[:, tt * 128:tt * 128 + 64], AF.Copy),
                    waits=[W("dve")] if tt == 0 else [])
                act(lambda e, tt=tt: e.activation(
                    vxB(tt)[:, 0:64], psB[:, tt * 128 + 64:(tt + 1) * 128], AF.Copy))

            hl0_fl = None
            for hl in range(2):
                hh = h0 + hl
                qxv = qx(hl)
                kxv = kx(hl)
                vxv = vxA if hl == 0 else vxB
                SHW = SH1 if hl == 0 else SH2
                shs = "d_s1" if hl == 0 else "d_s2"
                avs = hl * 512
                dinv_f = scr[0:1, 0:T]
                dtmp = scr[0:1, 2048:2048 + T]
                expw = {}
                drain_act = {}
                casts = [None] * NSUP

                def emit_s(isup, jj, hl=hl, qxv=qxv, kxv=kxv, SHW=SHW, shs=shs,
                           expw=expw, hh=hh):
                    slot = jj % 4
                    jmax = 4 * isup + 3
                    offc = max(0, jj * 128 - isup * 512)
                    has_diag = (isup * 4 <= jj <= jmax)
                    has_off = (isup * 4 <= jj + 1 <= jmax)
                    w = []
                    if jj == 0 and isup == 0:
                        w += [(sem[shs], SHW), (sem["dve"], QKX)]
                        if last_expall is not None:
                            w.append((sem["act"], last_expall))
                    if jj >= 4:
                        w.append((sem["act"], expw[(isup, jj - 4)]))
                    elif isup > 0:
                        w.append((sem["act"],
                                  expw[(isup - 1, 4 * (isup - 1) + slot)]))
                    pe(lambda e, jj=jj, slot=slot, offc=offc, isup=isup,
                       hd=has_diag, ho=has_off, qxv=qxv, kxv=kxv: e.matmul(
                        psA[:, slot * 512 + offc:slot * 512 + 512],
                        kxv[:, jj * 128:(jj + 1) * 128],
                        qxv[:, isup * 512 + offc:(isup + 1) * 512],
                        start=True, stop=not (hd or ho)),
                       waits=w, inc=not (has_diag or has_off))
                    if has_diag:
                        co = jj * 128 - isup * 512
                        pe(lambda e, slot=slot, co=co, hh=hh, ho=has_off: e.matmul(
                            psA[:, slot * 512 + co:slot * 512 + co + 128],
                            c_id[:], biasd_s[:, hh * 128:(hh + 1) * 128],
                            start=False, stop=not ho),
                           inc=not has_off)
                    if has_off:
                        co = (jj + 1) * 128 - isup * 512
                        pe(lambda e, slot=slot, co=co, hh=hh: e.matmul(
                            psA[:, slot * 512 + co:slot * 512 + co + 128],
                            c_id[:], biaso_s[:, hh * 128:(hh + 1) * 128],
                            start=False, stop=True),
                           inc=True)
                    MMD = sem["pe"].n
                    expw[(isup, jj)] = act(
                        lambda e, jj=jj, slot=slot, offc=offc: e.activation(
                            attnT(jj)[:, offc:512],
                            psA[:, slot * 512 + offc:slot * 512 + 512], AF.Exp),
                        waits=[(sem["pe"], MMD)])

                def emit_av(isup, jj, vxv=vxv, avs=avs, expw=expw,
                            drain_act=drain_act):
                    jmax = 4 * isup + 3
                    w = [(sem["act"], expw[(isup, jj)])]
                    if jj == 0:
                        if isup > 0:
                            w.append((sem["act"], drain_act[isup - 1]))
                        elif prev_mults is not None and avs // 512 in prev_mults:
                            w.append((sem["dve"], prev_mults[avs // 512]))
                    offc = max(0, jj * 128 - isup * 512)
                    pe(lambda e, jj=jj, avs=avs, vxv=vxv, jmax=jmax, offc=offc:
                       e.matmul(
                        psB[0:65, avs + offc:avs + 512], vxv(jj),
                        attnT(jj)[:, offc:512],
                        start=(jj == 0), stop=(jj == jmax)),
                       waits=w, inc=(jj == jmax))

                for isup in range(NSUP):
                    jmax = 4 * isup + 3
                    emit_s(isup, 0)
                    if hl == 1 and isup == 0:
                        p1 = [s for s in range(NSL - 1) if s != hl0_fl["sl_own"]]
                        if hl0_fl["sl_own"] < NSL - 1:
                            p1.append(hl0_fl["sl_own"])
                        flush_part(hl0_fl, p1)
                    for jj in range(1, jmax + 1):
                        emit_s(isup, jj)
                        if hl == 1 and isup == 0 and jj == 1:
                            nxt = avs // 512
                            mo = sorted(range(NSL),
                                        key=lambda s: 0 if s == nxt else 1 + s)
                            flush_part(hl0_fl, [NSL - 1], mult_order=mo)
                            prev_mults = hl0_fl["mults"]
                        if jj >= 2:
                            emit_av(isup, jj - 2)
                    emit_av(isup, jmax - 1)
                    emit_av(isup, jmax)
                    AVD = sem["pe"].n
                    act(lambda e, hh=hh, hl=hl, isup=isup, avs=avs: e.activation(
                        ynT[hl * 64:hl * 64 + 64,
                            (hh // 2) * T + isup * 512:
                            (hh // 2) * T + (isup + 1) * 512],
                        psB[0:64, avs:avs + 512], AF.Copy),
                        waits=[(sem["pe"], AVD)])
                    act(lambda e, isup=isup, avs=avs, dtmp=dtmp: e.activation(
                        dtmp[0:1, isup * 512:(isup + 1) * 512],
                        psB[64:65, avs:avs + 512], AF.Copy),
                        waits=([(sem["dve"], last_casts[isup])]
                               if last_casts is not None else []))
                    drain_act[isup] = sem["act"].n
                    dve(lambda e, isup=isup, dinv_f=dinv_f, dtmp=dtmp:
                        e.reciprocal(
                            dinv_f[0:1, isup * 512:(isup + 1) * 512],
                            dtmp[0:1, isup * 512:(isup + 1) * 512]),
                        waits=[(sem["act"], drain_act[isup])])
                    casts[isup] = dve(
                        lambda e, isup=isup, hl=hl, dinv_f=dinv_f: e.tensor_copy(
                            sqb[32 * hl:32 * hl + 1,
                                isup * 512:(isup + 1) * 512],
                            dinv_f[0:1, isup * 512:(isup + 1) * 512]))
                last_expall = expw[(NSUP - 1, 4 * (NSUP - 1) + 3)]
                last_casts = casts
                fl = {"hl": hl, "hg": g, "sl_own": avs // 512, "cast": casts,
                      "drain3": drain_act[NSUP - 1], "bc_pe": {}, "mults": {},
                      "h0drain3": hl0_fl["drain3"] if hl == 1 else None,
                      "mlast": None}
                if hl == 0:
                    hl0_fl = fl
                elif g == NG - 1:
                    p1 = [s for s in range(NSL - 1) if s != fl["sl_own"]]
                    if fl["sl_own"] < NSL - 1:
                        p1.append(fl["sl_own"])
                    flush_part(fl, p1)
                    mo = sorted(range(NSL), key=lambda s: 0 if s == 0 else 1 + s)
                    flush_part(fl, [NSL - 1], mult_order=mo)
                    prev_mults = fl["mults"]
                    prev_mlast = fl["mlast"]
                else:
                    prev_flush = fl
        ATT_DVE = sem["dve"].n
        ATT_PE = sem["pe"].n
        ATT_ACT = sem["act"].n
        if dbg:
            dbg_xn = dint("dbg_xn", [128, 8 * T], BF16)
            dbg_yn = dint("dbg_yn", [128, 8 * T], BF16)
            dbg_q = dint("dbg_q", [65, 2 * T], BF16)
            dbg_k = dint("dbg_k", [65, 2 * T], BF16)
            dmaq("sync", "d_c", lambda e: e.dma_start(out=dbg_xn[:], in_=bigB[:, 0:8 * T]),
                 waits=[(sem["dve"], ATT_DVE), (sem["act"], ATT_ACT), (sem["pe"], ATT_PE)])
            dmaq("sync", "d_c", lambda e: e.dma_start(out=dbg_yn[:], in_=ynT[:, 0:8 * T]))
            dmaq("sync", "d_c", lambda e: e.dma_start(out=dbg_q[:], in_=bigA[0:65, 0:2 * T]))
            dmaq("sync", "d_c", lambda e: e.dma_start(out=dbg_k[:], in_=bigA[0:65, 2 * T:4 * T]))

        wo_col = lambda dc: (3072 + dc * 1024) if dc < 5 else (dc - 5) * 1024
        xldc = [scr[:, 4096 + i * 1024:4096 + (i + 1) * 1024] for i in range(2)]
        stile = [scr[:, i * 1024:(i + 1) * 1024] for i in range(2)]
        x1f32 = scr[:, 2048:3072]
        x1st = [sqb[:, 0:1024], sqb[:, 1024:2048]]
        dxst = [bigA[:, QPK0 + 2 * T + i * 1024:QPK0 + 2 * T + (i + 1) * 1024]
                for i in range(2)] if QPK0 + 2 * T + 2048 <= WA else \
               [bigA[:, 0:1024], bigA[:, 1024:2048]]
        s_done = []
        ag_done = []
        dx_done = []
        for tt in range(TT):
            slot = tt % 2
            xs = "d_x0" if slot == 0 else "d_x1"
            ags = "d_ta" if slot == 0 else "d_tb"
            dxs = "d_oa" if slot == 0 else "d_ob"
            ps = psA if slot == 0 else psB
            w = [(sem["dve"], s_done[tt - 2])] if tt >= 2 else             [(sem["dve"], ATT_DVE), (sem["act"], ATT_ACT),
                 (sem["d_s1"], sem["d_s1"].n), (sem["d_s2"], sem["d_s2"].n)]
            XLD = dmaq("sync", xs, lambda e, tt=tt, slot=slot: e.dma_start(
                out=xldc[slot][:], in_=xtm[tt * 128:(tt + 1) * 128, :]), waits=w)
            w0 = [(sem["d_wo"], WOLD), (sem["dve"], ATT_DVE), (sem["act"], ATT_ACT)] \
                if tt == 0 else []
            if tt >= 2:
                w0 = w0 + [(sem["dve"], s_done[tt - 2])]
            for dc in range(DC):
                for sl in range(2):
                    pe(lambda e, tt=tt, dc=dc, ps=ps, sl=sl: e.matmul(
                        ps[:, sl * 512:(sl + 1) * 512],
                        ynT[:, dc * T + tt * 128:dc * T + (tt + 1) * 128],
                        wb[:, wo_col(dc) + sl * 512:wo_col(dc) + (sl + 1) * 512],
                        start=(dc == 0), stop=(dc == DC - 1)),
                       waits=w0 if dc == 0 and sl == 0 else [],
                       inc=(dc == DC - 1 and sl == 1))
            OMM = sem["pe"].n
            sw = [(sem["pe"], OMM), (sem[xs], XLD)]
            if tt >= 2:
                sw.append((sem[ags], ag_done[tt - 2]))
                sw.append((sem[dxs], dx_done[tt - 2]))
            s_done.append(dve(lambda e, slot=slot, ps=ps: e.scalar_tensor_tensor(
                stile[slot][:], xldc[slot][:], 2.0, ps[:, 0:1024], ALU.mult, ALU.add,
                accum_out=smc[:, 0:1]), waits=sw))
            SQ2 = act(lambda e, slot=slot: e.activation(
                x1st[slot][:], stile[slot][:], AF.Square, accum_out=smc[:, 1:2]),
                waits=[W("dve")])
            dve(lambda e: e.tensor_scalar(smc[:, 0:1], smc[:, 0:1], 1.0 / D, None, ALU.mult),
                waits=[(sem["act"], SQ2)])
            dve(lambda e: e.tensor_scalar(smc[:, 1:2], smc[:, 1:2], 1.0 / D, None, ALU.mult))
            dve(lambda e: e.tensor_tensor(smc[:, 3:4], smc[:, 0:1], smc[:, 0:1], ALU.mult))
            VAR2 = dve(lambda e: e.tensor_tensor(smc[:, 1:2], smc[:, 1:2], smc[:, 3:4],
                                                 ALU.subtract))
            act(lambda e: e.activation(smc[:, 4:5], smc[:, 1:2], AF.Ln, bias=epsc[:, 0:1]),
                waits=[(sem["dve"], VAR2)])
            RS2 = act(lambda e: e.activation(smc[:, 2:3], smc[:, 4:5], AF.Exp, scale=-0.5))
            dve(lambda e, slot=slot: e.tensor_scalar(
                x1f32[:], stile[slot][:], smc[:, 0:1], smc[:, 2:3],
                ALU.subtract, ALU.mult), waits=[(sem["act"], RS2)])
            X1BF = act(lambda e, slot=slot: e.activation(x1st[slot][:], x1f32[:], AF.Copy),
                       waits=[W("dve")])
            DXD = dve(lambda e, slot=slot: e.tensor_tensor(
                dxst[slot][:], x1f32[:], x1st[slot][:], ALU.subtract),
                waits=[(sem["act"], X1BF)])
            ag_done.append(dmaq("sync", ags, lambda e, tt=tt, slot=slot: e.dma_start(
                out=ag_in[tt * 128:(tt + 1) * 128, :], in_=x1st[slot][:]),
                waits=[(sem["act"], X1BF)]))
            dx_done.append(dmaq("sync", dxs, lambda e, tt=tt, slot=slot: e.dma_start(
                out=dx_dram[tt * 128:(tt + 1) * 128, :], in_=dxst[slot][:]),
                waits=[(sem["dve"], DXD)]))
        X1AG = sem["d_ta"].n
        X1BG = sem["d_tb"].n
        X1OA = sem["d_oa"].n
        X1OB = sem["d_ob"].n

    if do_coll:
        emit("gpsimd", lambda e: e.collective_compute(
            "AllGather", ALU.bypass, replica_groups=[list(range(NCORES))],
            ins=[ag_in[:]], outs=[x1_full[:]]),
            inc=sem["coll"], inck=16,
            waits=[(sem["d_ta"], X1AG), (sem["d_tb"], X1BG),
                   (sem["d_oa"], X1OA), (sem["d_ob"], X1OB)])
        AGX1 = sem["coll"].n
    if do_a:

        lgmm_done = []
        for dc in range(DC):
            trs = "d_ta" if dc % 2 == 0 else "d_tb"
            wtr = [(sem["d_ta"], X1AG), (sem["d_tb"], X1BG),
                   (sem["d_oa"], X1OA), (sem["d_ob"], X1OB)] if dc < 2 else               [(sem["pe"], lgmm_done[dc - 2])]
            ab0 = (dc % 2) * 2 * T
            dmaq("sync", trs, lambda e, dc=dc, ab0=ab0: e.dma_start(
                out=atb[:, ab0:ab0 + T], in_=ag_in[:, dc * 128:(dc + 1) * 128],
                transpose=True), waits=wtr)
            TRD = dmaq("sync", trs, lambda e, dc=dc, ab0=ab0: e.dma_start(
                out=atb[:, ab0 + T:ab0 + 2 * T], in_=dx_dram[:, dc * 128:(dc + 1) * 128],
                transpose=True))
            for tt in range(TT):
                pe(lambda e, dc=dc, tt=tt, ab0=ab0: e.matmul(
                    psA[:, tt * 8:(tt + 1) * 8], atb[:, ab0 + tt * 128:ab0 + (tt + 1) * 128],
                    rw_s[:, dc * E:(dc + 1) * E], start=True, stop=False),
                   waits=([(sem[trs], TRD), (sem["d_rw"], RWLD)] +
                          ([W("dve")] if dc >= 1 else [])) if tt == 0 else [],
                   inc=False)
                pe(lambda e, dc=dc, tt=tt, ab0=ab0: e.matmul(
                    psA[:, tt * 8:(tt + 1) * 8],
                    atb[:, ab0 + T + tt * 128:ab0 + T + (tt + 1) * 128],
                    rw_s[:, dc * E:(dc + 1) * E], start=False, stop=True),
                   inc=(tt == TT - 1))
            lgmm_done.append(sem["pe"].n)
            if dc == 0:
                dve(lambda e: e.tensor_copy(scr[:, 1024:1024 + TT * 8],
                                            psA[:, 0:TT * 8]),
                    waits=[(sem["pe"], lgmm_done[-1])])
            else:
                dve(lambda e: e.tensor_tensor(scr[:, 1024:1024 + TT * 8],
                                              scr[:, 1024:1024 + TT * 8],
                                              psA[:, 0:TT * 8], ALU.add),
                    waits=[(sem["pe"], lgmm_done[-1])])
        LGMM = sem["pe"].n
        dve(lambda e: e.tensor_tensor(scr[:, 1024:1024 + TT * 8],
                                      scr[:, 1024:1024 + TT * 8],
                                      rbb_s[:].rearrange("p e -> p one e", one=1)
                                      .broadcast(1, TT) if False else rbb_s[:],
                                      ALU.add)) if False else None
        for tt in range(TT):
            dve(lambda e, tt=tt: e.tensor_tensor(
                scr[:, 1024 + tt * 8:1024 + (tt + 1) * 8],
                scr[:, 1024 + tt * 8:1024 + (tt + 1) * 8], rbb_s[:], ALU.add))
        LGE = sem["dve"].n
        LGST = dmaq("sync", "d_t", lambda e: e.dma_start(
            out=lg_in.rearrange("(c p) e -> p c e", p=128),
            in_=scr[:, 1024:1024 + TT * 8].rearrange("p (c e) -> p c e", e=E)),
            waits=[(sem["dve"], LGE)])
    if do_coll:
        emit("gpsimd", lambda e: e.collective_compute(
            "AllGather", ALU.bypass, replica_groups=[list(range(NCORES))],
            ins=[lg_in[:]], outs=[lg_full[:]]),
            inc=sem["coll"], inck=16, waits=[(sem["d_t"], LGST)])
        AGLG = sem["coll"].n
    if do_b:
        SE_ = S * E
        YN_PE = sem["pe"].n
        YN_DVE = sem["dve"].n
        wsl = [wb[:, i * 1024:(i + 1) * 1024] for i in range(2)]
        vsl = [wb[:, 2048 + i * 1024:2048 + (i + 1) * 1024] for i in range(2)]
        yst = [wb[:, 4096 + i * 1024:4096 + (i + 1) * 1024] for i in range(2)]
        WOL = dmaq("gpsimd", "d_w", lambda e: e.dma_start(
            out=ynT[:, 0:HC * 1024].rearrange("p (c o) -> p c o", o=1024),
            in_=WoutE.rearrange("(c p) o -> p c o", p=128)),
            waits=[(sem["pe"], YN_PE), (sem["dve"], YN_DVE)])
        pre_wvl = {}
        for hc in range(2):
            wvs = "d_wa" if hc == 0 else "d_wb"
            dmaq("gpsimd", wvs, lambda e, hc=hc: e.dma_start(
                out=wsl[hc][:].rearrange("p (c h) -> p c h", h=128),
                in_=We[:, hc * 128:(hc + 1) * 128].rearrange("(c p) h -> p c h", p=128)))
            pre_wvl[hc] = dmaq("gpsimd", wvs, lambda e, hc=hc: e.dma_start(
                out=vsl[hc][:].rearrange("p (c h) -> p c h", h=128),
                in_=Ve[:, hc * 128:(hc + 1) * 128].rearrange("(c p) h -> p c h", p=128)))
        lgt = scr[:, 0:SE_]
        cumf = scr[:, SE_:2 * SE_]
        lg2 = scr[:, 2 * SE_:3 * SE_]
        base = 3 * SE_

        def rvv(i):
            return scr[:, base + i * S:base + (i + 1) * S]
        v1, v2, id1, id2, w1, w2, t1, t2, t3, t4 = (rvv(i) for i in range(10))

        LGF = dmaq("sync", "d_r", lambda e: e.dma_start(
            out=lgt.rearrange("p (s e) -> p s e", e=E),
            in_=lg_full.rearrange("(s p) e -> p s e", p=128)),
            waits=[(sem["coll"], AGLG)])
        lgt3 = lgt.rearrange("p (s e) -> p s e", e=E)
        lgte = lambda ee: lgt.rearrange("p (s e) -> p e s", e=E)[:, ee, :]
        lg2e = lambda ee: lg2.rearrange("p (s e) -> p e s", e=E)[:, ee, :]
        dve(lambda e: e.reduce_max(v1, lgt3, axis=AX.X),
            waits=[(sem["d_r"], LGF), W("d_c")])
        dve(lambda e: e.memset(id1, 1e9))
        for ee in range(E):
            dve(lambda e, ee=ee: e.tensor_tensor(t1, lgte(ee), v1, ALU.is_ge))
            dve(lambda e, ee=ee: e.tensor_scalar(t1, t1, -1e6, 1e6 + ee, ALU.mult, ALU.add))
            dve(lambda e: e.tensor_tensor(id1, id1, t1, ALU.min))
        dve(lambda e: e.tensor_copy(lg2, lgt))
        for ee in range(E):
            dve(lambda e, ee=ee: e.tensor_scalar(t1, id1, float(ee), None, ALU.is_equal))
            dve(lambda e: e.tensor_scalar(t1, t1, -1e30, None, ALU.mult))
            dve(lambda e, ee=ee: e.tensor_tensor(lg2e(ee), lg2e(ee), t1, ALU.add))
        lg23 = lg2.rearrange("p (s e) -> p s e", e=E)
        dve(lambda e: e.reduce_max(v2, lg23, axis=AX.X))
        dve(lambda e: e.memset(id2, 1e9))
        for ee in range(E):
            dve(lambda e, ee=ee: e.tensor_tensor(t1, lg2e(ee), v2, ALU.is_ge))
            dve(lambda e, ee=ee: e.tensor_scalar(t1, t1, -1e6, 1e6 + ee, ALU.mult, ALU.add))
            dve(lambda e: e.tensor_tensor(id2, id2, t1, ALU.min))
        dve(lambda e: e.tensor_tensor(t1, v1, v2, ALU.subtract))
        SGW = act(lambda e: e.activation(w1, t1, AF.Sigmoid), waits=[W("dve")])
        dve(lambda e: e.tensor_scalar(w2, w1, -1.0, 1.0, ALU.mult, ALU.add),
            waits=[(sem["act"], SGW)])
        for ee in range(E):
            dve(lambda e, ee=ee: e.tensor_scalar(t1, id1, float(ee), None, ALU.is_equal))
            dve(lambda e, ee=ee: e.tensor_scalar(t2, id2, float(ee), None, ALU.is_equal))
            dve(lambda e, ee=ee: e.tensor_tensor(
                sqb[:, ee * S:(ee + 1) * S], t1, t2, ALU.add))
        AIND = sem["dve"].n
        for ee in range(E):
            pe(lambda e, ee=ee: e.matmul(
                psA[:, ee * S:(ee + 1) * S], c_uti[:], sqb[:, ee * S:(ee + 1) * S],
                start=True, stop=True),
               waits=[(sem["dve"], AIND), W("act"), W("d_c")] if ee == 0 else [],
               inc=False)
            pe(lambda e, ee=ee: e.matmul(
                psB[0:1, ee * S:(ee + 1) * S], c_o128[:], sqb[:, ee * S:(ee + 1) * S],
                start=True, stop=True),
               inc=(ee == E - 1))
        CPFX = sem["pe"].n
        trow = lg2[0:1, 0:SE_]
        dve(lambda e: e.tensor_copy(trow, psB[0:1, 0:SE_]), waits=[(sem["pe"], CPFX)])
        t3r = lambda ap: ap.rearrange("one (e s) -> one e s", s=S)
        rowA = trow
        rowB = lgt[0:1, 0:SE_]
        d = 1
        cur, oth = rowA, rowB
        while d < S:
            dve(lambda e, cur=cur, oth=oth, d=d: e.tensor_copy(
                t3r(oth)[:, :, 0:d], t3r(cur)[:, :, 0:d]))
            dve(lambda e, cur=cur, oth=oth, d=d: e.tensor_tensor(
                t3r(oth)[:, :, d:S], t3r(cur)[:, :, d:S], t3r(cur)[:, :, 0:S - d], ALU.add))
            cur, oth = oth, cur
            d *= 2
        dve(lambda e, oth=oth: e.tensor_copy(oth, psB[0:1, 0:SE_]))
        EXCR = dve(lambda e, cur=cur, oth=oth: e.tensor_tensor(cur, cur, oth, ALU.subtract))
        for ee in range(E):
            pe(lambda e, ee=ee, cur=cur: e.matmul(
                psB[:, ee * S:(ee + 1) * S], c_o1f[:], cur[0:1, ee * S:(ee + 1) * S],
                start=True, stop=True),
               waits=[(sem["dve"], EXCR)] if ee == 0 else [],
               inc=(ee == E - 1))
        CBC = sem["pe"].n
        dve(lambda e: e.tensor_copy(cumf, psB[:, 0:SE_]), waits=[(sem["pe"], CBC)])
        dve(lambda e: e.tensor_tensor(cumf, cumf, psA[:, 0:SE_], ALU.add))
        cum3 = lambda ee: cumf.rearrange("p (e s) -> p e s", s=S)[:, ee, :]
        dve(lambda e: e.memset(t3, 0.0))
        dve(lambda e: e.memset(t4, 0.0))
        for ee in range(E):
            dve(lambda e, ee=ee: e.tensor_scalar(t1, id1, float(ee), None, ALU.is_equal))
            dve(lambda e, ee=ee: e.tensor_tensor(t1, t1, cum3(ee), ALU.mult))
            dve(lambda e: e.tensor_tensor(t3, t3, t1, ALU.add))
            dve(lambda e, ee=ee: e.tensor_scalar(t2, id2, float(ee), None, ALU.is_equal))
            dve(lambda e, ee=ee: e.tensor_tensor(t2, t2, cum3(ee), ALU.mult))
            dve(lambda e: e.tensor_tensor(t4, t4, t2, ALU.add))
        dve(lambda e: e.tensor_scalar(v1, t3, CAP + 0.5, None, ALU.is_le))
        dve(lambda e: e.tensor_scalar(v2, t4, CAP + 0.5, None, ALU.is_le))

        def build_idx(idsl, sel, kk, dsti):
            dve(lambda e, idsl=idsl: e.tensor_scalar(t1, idsl, float(CAP), None, ALU.mult))
            dve(lambda e, sel=sel: e.tensor_tensor(t1, t1, sel, ALU.add))
            dve(lambda e: e.tensor_scalar(t1, t1, -1.0, None, ALU.add))
            dve(lambda e, kk=kk: e.tensor_tensor(t1, t1, kk, ALU.mult))
            return dve(lambda e, dsti=dsti: e.tensor_copy(dsti[:], t1))
        IX1D = build_idx(id1, t3, v1, ix1_s)
        IX2D = build_idx(id2, t4, v2, ix2_s)
        dmaq("sync", "d_ks", lambda e: e.dma_start(out=idx1_dram[:], in_=ix1_s[:]),
             waits=[(sem["dve"], IX1D)])
        dmaq("sync", "d_ks", lambda e: e.dma_start(out=idx2_dram[:], in_=ix2_s[:]),
             waits=[(sem["dve"], IX2D)])
        dmaq("sync", "d_ks", lambda e: e.dma_start(out=k1_dram[:], in_=v1),
             waits=[(sem["dve"], IX2D)])
        KSALL = dmaq("sync", "d_ks", lambda e: e.dma_start(out=k2_dram[:], in_=v2))
        emit("gpsimd", lambda e: e.dma_start(
            out=kst[:, 0:SP], in_=k1_dram[:, bass.ts(e.partition_id(), SP)]),
            inc=sem["d_ix"], inck=16, waits=[(sem["d_ks"], KSALL)])
        emit("gpsimd", lambda e: e.dma_start(
            out=kst[:, SP:2 * SP], in_=k2_dram[:, bass.ts(e.partition_id(), SP)]),
            inc=sem["d_ix"], inck=16)
        emit("gpsimd", lambda e: e.dma_start(
            out=ixst[:, 0:SP], in_=idx1_dram[:, bass.ts(e.partition_id(), SP)]),
            inc=sem["d_ix"], inck=16)
        emit("gpsimd", lambda e: e.dma_start(
            out=ixst[:, SP:2 * SP], in_=idx2_dram[:, bass.ts(e.partition_id(), SP)]),
            inc=sem["d_ix"], inck=16)
        IXALL = sem["d_ix"].n
        if part == "b":
            dmaq("sync", "d_t", lambda e: e.dma_start(out=ixst_out[:], in_=ixst[:, 0:2 * SP]),
                 waits=[(sem["d_ix"], IXALL)])
            dmaq("sync", "d_t", lambda e: e.dma_start(out=kst_out[:], in_=kst[:, 0:2 * SP]))

        dve(lambda e: e.memset(t4, 0.0))
        for ee in range(E):
            dve(lambda e, ee=ee: e.tensor_scalar(smc[:, 4:5], eid_s[:], float(ee), None,
                                                 ALU.is_equal))
            dve(lambda e, ee=ee: e.tensor_scalar(t3, cum3(ee), smc[:, 4:5], None, ALU.mult))
            dve(lambda e: e.tensor_tensor(t4, t4, t3, ALU.add))
        dve(lambda e: e.tensor_scalar(t1, id1, eid_s[:, 0:1], None, ALU.is_equal))
        dve(lambda e: e.tensor_scalar(t2, id2, eid_s[:, 0:1], None, ALU.is_equal))
        dve(lambda e: e.tensor_tensor(t1, t1, t2, ALU.add))
        dve(lambda e: e.tensor_scalar(t3, t4, CAP + 0.5, None, ALU.is_le))
        dve(lambda e: e.tensor_tensor(t1, t1, t3, ALU.mult))
        dve(lambda e: e.tensor_scalar(t3, t4, -1.0, None, ALU.add))
        dve(lambda e: e.tensor_tensor(t3, t3, t1, ALU.mult))
        dve(lambda e: e.tensor_scalar(t2, t1, -1.0, 1.0, ALU.mult, ALU.add))
        dve(lambda e: e.tensor_scalar(t4, c_cid2[:], float(CAP), None, ALU.add),
            waits=[W("d_c")])
        dve(lambda e: e.tensor_tensor(t2, t2, t4, ALU.mult))
        dve(lambda e: e.tensor_tensor(t3, t3, t2, ALU.add))
        OFF1 = dve(lambda e: e.tensor_copy(off_s[:], t3))
        dve(lambda e: e.tensor_copy(tw_s[:], c_tw[:]), waits=[W("d_c")])
        dve(lambda e: e.tensor_scalar(t2, id1, eid_s[:, 0:1], None, ALU.is_equal))
        dve(lambda e: e.tensor_tensor(t2, t2, w1, ALU.mult))
        dve(lambda e: e.tensor_scalar(t4, id2, eid_s[:, 0:1], None, ALU.is_equal))
        dve(lambda e: e.tensor_tensor(t4, t4, w2, ALU.mult))
        dve(lambda e: e.tensor_tensor(t2, t2, t4, ALU.add))
        dve(lambda e: e.tensor_tensor(t2, t2, t1, ALU.mult))
        WSEL = dve(lambda e: e.tensor_copy(
            tw_s[:, 0:2 * S].rearrange("p (s two) -> p s two", two=2)[:, :, 1:2],
            t2.rearrange("p (s one) -> p s one", one=1)))
        GWI = dmaq("sync", "d_gw", lambda e: e.dma_start(
            out=gw_dram[:, 0:2].rearrange("(c p) two -> p c two", p=128),
            in_=c_gwi[:, 0:2 * (CT + 1)].rearrange("p (c two) -> p c two", two=2)),
            waits=[W("d_c")])
        for sc in range(S):
            dmaq("gpsimd", "d_sc", lambda e, sc=sc: e.indirect_dma_start(
                gw_dram[:], IndirectOffsetOnAxis(off_s[:, sc:sc + 1], 0),
                tw_s[:, 2 * sc:2 * sc + 128], None),
                waits=[(sem["d_gw"], GWI), (sem["dve"], max(OFF1, WSEL))] if sc == 0 else [])
        GWSC = sem["d_sc"].n
    if do_b:
        xinP = bigA[:, 0:CT * 1024]
        MSX = dve(lambda e: e.memset(xinP, 0.0), waits=[(sem["pe"], sem["pe"].n), W("d_c")])
        gwq = ["sync", "scalar", "sync", "scalar"]
        CQ = max(CT // 4, 1)
        for kq in range(CT // CQ):
            dmaq(gwq[kq % 4], "d_gw", lambda e, kq=kq: e.dma_start(
                out=gwt_s[:, 2 * kq * CQ:2 * (kq + 1) * CQ].rearrange(
                    "p (c two) -> p c two", two=2),
                in_=gw_dram[kq * CQ * 128:(kq + 1) * CQ * 128, 0:2].rearrange(
                    "(c p) two -> p c two", p=128)),
                waits=[(sem["d_sc"], GWSC)])
        GWL = sem["d_gw"].n
        for jb in range(CT):
            dve(lambda e, jb=jb: e.tensor_copy(gcol_i[:, jb:jb + 1],
                                               gwt_s[:, 2 * jb:2 * jb + 1]),
                waits=[(sem["d_gw"], GWL)] if jb == 0 else [])
        GCI = sem["dve"].n
        xinT = lambda dc: bigB[:, dc * CAP:(dc + 1) * CAP]
        gaw = []
        tpd = {}
        xdr = {}
        gq = lambda jb: "d_gc" if jb % 2 == 0 else "d_gd"

        def emit_gather(jb):
            w = []
            if jb == 0:
                w = [(sem["dve"], GCI), (sem["dve"], MSX), (sem["coll"], AGX1)]
            if jb >= 2:
                w = w + [(sem["pe"], tpd[jb - 2])]
            gaw.append(dmaq("gpsimd", gq(jb), lambda e, jb=jb: e.indirect_dma_start(
                xinP[:, jb * 1024:(jb + 1) * 1024], None,
                x1_full[:], IndirectOffsetOnAxis(gcol_i[:, jb:jb + 1], 0)),
                waits=w))

        emit_gather(0)
        emit_gather(1)
        for jb in range(CT):
            pb = (jb % 2) * 1024
            for dcc in range(DC):
                w = []
                if dcc == 0:
                    w = [(sem[gq(jb)], gaw[jb])]
                    if jb == 0:
                        w.append(W("dve"))
                    if jb >= 2:
                        w.append((sem["act"], xdr[jb - 2]))
                pe(lambda e, jb=jb, dcc=dcc, pb=pb: e.matmul(
                    psB[:, pb + dcc * 128:pb + (dcc + 1) * 128],
                    xinP[:, jb * 1024 + dcc * 128:jb * 1024 + (dcc + 1) * 128],
                    c_id[:], start=True, stop=True),
                   waits=w, inc=(dcc == DC - 1))
            tpd[jb] = sem["pe"].n
            if jb + 2 < CT:
                emit_gather(jb + 2)
            for dcc in range(DC):
                act(lambda e, jb=jb, dcc=dcc, pb=pb: e.activation(
                    bigB[:, dcc * CAP + jb * 128:dcc * CAP + (jb + 1) * 128],
                    psB[:, pb + dcc * 128:pb + (dcc + 1) * 128], AF.Copy),
                    waits=[(sem["pe"], tpd[jb])] if dcc == 0 else [])
            xdr[jb] = sem["act"].n
        XTRA = sem["act"].n
        woutc = lambda hc: ynT[:, hc * 1024:(hc + 1) * 1024]
        hrg = lambda hc: bigA[:, hc * TB:(hc + 1) * TB]
        ab_done = {}
        ys_done = []
        for tb in range(NTB):
            for hc in range(HC):
                slot = hc % 2
                w = [W("pe")] if (tb == 0 and hc < 2) else []
                prev = (tb, hc - 2) if hc >= 2 else (tb - 1, HC - 2 + hc)
                if prev in ab_done:
                    w = [(sem["pe"], ab_done[prev])]
                wvs = "d_wa" if slot == 0 else "d_wb"
                if tb == 0 and hc < 2:
                    WVL = pre_wvl[hc]
                else:
                    dmaq("gpsimd", wvs, lambda e, hc=hc, slot=slot: e.dma_start(
                        out=wsl[slot][:].rearrange("p (c h) -> p c h", h=128),
                        in_=We[:, hc * 128:(hc + 1) * 128].rearrange("(c p) h -> p c h", p=128)),
                        waits=w)
                    WVL = dmaq("gpsimd", wvs, lambda e, hc=hc, slot=slot: e.dma_start(
                        out=vsl[slot][:].rearrange("p (c h) -> p c h", h=128),
                        in_=Ve[:, hc * 128:(hc + 1) * 128].rearrange("(c p) h -> p c h", p=128)))
                NSL = TB // 512
                for dc in range(DC):
                    for sl in range(NSL):
                        pe(lambda e, dc=dc, slot=slot, tb=tb, sl=sl: e.matmul(
                            psA[:, sl * 512:(sl + 1) * 512],
                            wsl[slot][:, dc * 128:(dc + 1) * 128],
                            xinT(dc)[:, tb * TB + sl * 512:tb * TB + (sl + 1) * 512],
                            start=(dc == 0), stop=(dc == DC - 1)),
                           waits=[(sem[wvs], WVL), (sem["act"], XTRA), W("dve")]
                           if dc == 0 and sl == 0 else [], inc=False)
                for dc in range(DC):
                    for sl in range(NSL):
                        pe(lambda e, dc=dc, slot=slot, tb=tb, sl=sl: e.matmul(
                            psA[:, TB + sl * 512:TB + (sl + 1) * 512],
                            vsl[slot][:, dc * 128:(dc + 1) * 128],
                            xinT(dc)[:, tb * TB + sl * 512:tb * TB + (sl + 1) * 512],
                            start=(dc == 0), stop=(dc == DC - 1)),
                           inc=(dc == DC - 1 and sl == NSL - 1))
                ABMM = sem["pe"].n
                ab_done[(tb, hc)] = ABMM
                SIL = act(lambda e: e.activation(sqb[:, 0:TB], psA[:, TB:2 * TB], AF.Silu),
                          waits=[(sem["pe"], ABMM)])
                dve(lambda e, hc=hc: e.tensor_tensor(hrg(hc), psA[:, 0:TB], sqb[:, 0:TB],
                                                     ALU.mult),
                    waits=[(sem["act"], SIL)])
            HD = sem["dve"].n
            for ttl in range(TB // 128):
                gt = tb * (TB // 128) + ttl
                slot = ttl % 2
                po = slot * 1024
                w0 = [(sem["dve"], HD), (sem["d_w"], WOL)]
                if len(ys_done) >= 2:
                    w0.append((sem["dve"], ys_done[-2][0]))
                for hc in range(HC):
                    for sl in range(2):
                        pe(lambda e, hc=hc, ttl=ttl, po=po, sl=sl: e.matmul(
                            psB[:, po + sl * 512:po + (sl + 1) * 512],
                            bigA[:, hc * TB + ttl * 128:hc * TB + (ttl + 1) * 128],
                            woutc(hc)[:, sl * 512:(sl + 1) * 512],
                            start=(hc == 0), stop=(hc == HC - 1)),
                           waits=w0 if hc == 0 and sl == 0 else [],
                           inc=(hc == HC - 1 and sl == 1))
                YMM = sem["pe"].n
                oss = "d_oa" if slot == 0 else "d_ob"
                ysw = [(sem["pe"], YMM), (sem["d_gw"], GWL)]
                if len(ys_done) >= 2:
                    ysw.append((sem[oss], ys_done[-2][1]))
                YS = dve(lambda e, gt=gt, slot=slot, po=po: e.tensor_scalar(
                    yst[slot][:], psB[:, po:po + 1024],
                    gwt_s[:, 2 * gt + 1:2 * gt + 2], None, ALU.mult),
                    waits=ysw)
                YSD = dmaq("sync", oss, lambda e, gt=gt, slot=slot: e.dma_start(
                    out=Y_in[gt * 128:(gt + 1) * 128, :], in_=yst[slot][:]),
                    waits=[(sem["dve"], YS)])
                ys_done.append((YS, YSD))
        YSTA = sem["d_oa"].n
        YSTB = sem["d_ob"].n
    if do_coll:
        emit("gpsimd", lambda e: e.collective_compute(
            "AllGather", ALU.bypass, replica_groups=[list(range(NCORES))],
            ins=[Y_in[:]], outs=[Y_full[:]]),
            inc=sem["coll"], inck=16,
            waits=[(sem["d_oa"], YSTA), (sem["d_ob"], YSTB)])
        AGY = sem["coll"].n
    if part == "c":
        dmaq("sync", "d_ix", lambda e: e.dma_start(out=ixst[:, 0:2 * SP], in_=ixst_in[:]))
        dmaq("sync", "d_ix", lambda e: e.dma_start(out=kst[:, 0:2 * SP], in_=kst_in[:]))
        IXALL = sem["d_ix"].n
    if do_c:
        y1g = lambda tt: bigA[:, tt * 1024:(tt + 1) * 1024]
        y2g = lambda tt: bigB[:, tt * 1024:(tt + 1) * 1024]
        dve(lambda e: e.memset(bigA[:, 0:TT * 1024], 0.0),
            waits=[(sem["pe"], sem["pe"].n), W("d_c")])
        MSY = dve(lambda e: e.memset(bigB[:, 0:TT * 1024], 0.0))
        for tt in range(TT):
            dmaq("gpsimd", "d_gd", lambda e, tt=tt: e.indirect_dma_start(
                y1g(tt), None, Y_full[:], IndirectOffsetOnAxis(ixst[:, tt:tt + 1], 0)),
                waits=[(sem["dve"], MSY), (sem["coll"], AGY), (sem["d_ix"], IXALL)]
                if tt == 0 else [])
            dmaq("gpsimd", "d_gd", lambda e, tt=tt: e.indirect_dma_start(
                y2g(tt), None, Y_full[:],
                IndirectOffsetOnAxis(ixst[:, SP + tt:SP + tt + 1], 0)))
        YGA = sem["d_gd"].n
        x1ld = [ynT[:, i * 1024:(i + 1) * 1024] for i in range(2)]
        dxld = [ynT[:, 2048 + i * 1024:2048 + (i + 1) * 1024] for i in range(2)]
        s2s = [scr[:, 0:1024], scr[:, 1024:2048]]
        ots = [scr[:, 2048:3072], scr[:, 3072:4096]]
        add_done = []
        st_done = {}
        sqd = {}
        PEFIN = sem["pe"].n

        def g_stage_a(tt):
            slot = tt % 2
            xs = "d_x0" if slot == 0 else "d_x1"
            s2 = s2s[slot]
            w = [(sem["dve"], add_done[tt - 2])] if tt >= 2 else [(sem["pe"], PEFIN)]
            dmaq("sync", xs, lambda e, tt=tt, slot=slot: e.dma_start(
                out=x1ld[slot][:], in_=ag_in[tt * 128:(tt + 1) * 128, :]), waits=w)
            XL2 = dmaq("sync", xs, lambda e, tt=tt, slot=slot: e.dma_start(
                out=dxld[slot][:], in_=dx_dram[tt * 128:(tt + 1) * 128, :]))
            wv = [(sem[xs], XL2), (sem["d_gd"], YGA)]
            if tt == 0:
                wv.append(W("d_c"))
                wv.append((sem["d_ks"], KSALL))
            dve(lambda e, slot=slot, s2=s2: e.tensor_tensor(
                s2, x1ld[slot][:], dxld[slot][:], ALU.add), waits=wv)
            add_done.append(sem["dve"].n)
            dve(lambda e, tt=tt, s2=s2: e.scalar_tensor_tensor(
                s2, y1g(tt), kst[:, tt:tt + 1], s2, ALU.mult, ALU.add))
            dve(lambda e, tt=tt, s2=s2: e.scalar_tensor_tensor(
                s2, y2g(tt), kst[:, SP + tt:SP + tt + 1], s2, ALU.mult, ALU.add))
            dve(lambda e, s2=s2, sb8=8 * slot: e.reduce_sum(
                smc[:, sb8:sb8 + 1], s2, axis=AX.X))
            MUC = dve(lambda e, sb8=8 * slot: e.tensor_scalar(
                smc[:, sb8:sb8 + 1], smc[:, sb8:sb8 + 1], 1.0 / D, None, ALU.mult))
            act(lambda e, s2=s2, slot=slot, sb8=8 * slot: e.activation(
                sqb[:, slot * 1024:(slot + 1) * 1024], s2, AF.Square,
                accum_out=smc[:, sb8 + 1:sb8 + 2]),
                waits=[(sem["dve"], MUC)])
            sqd[tt] = sem["act"].n

        def g_stage_b(tt):
            slot = tt % 2
            oss = "d_oa" if slot == 0 else "d_ob"
            s2 = s2s[slot]
            ot = ots[slot]
            sb8 = 8 * slot
            dve(lambda e, sb8=sb8: e.tensor_scalar(
                smc[:, sb8 + 1:sb8 + 2], smc[:, sb8 + 1:sb8 + 2], 1.0 / D, None,
                ALU.mult), waits=[(sem["act"], sqd[tt])])
            dve(lambda e, sb8=sb8: e.tensor_tensor(
                smc[:, sb8 + 3:sb8 + 4], smc[:, sb8:sb8 + 1], smc[:, sb8:sb8 + 1],
                ALU.mult))
            VARG = dve(lambda e, sb8=sb8: e.tensor_tensor(
                smc[:, sb8 + 1:sb8 + 2], smc[:, sb8 + 1:sb8 + 2],
                smc[:, sb8 + 3:sb8 + 4], ALU.subtract))
            act(lambda e, sb8=sb8: e.activation(
                smc[:, sb8 + 4:sb8 + 5], smc[:, sb8 + 1:sb8 + 2], AF.Ln,
                bias=epsc[:, 0:1]), waits=[(sem["dve"], VARG)])
            RSG = act(lambda e, sb8=sb8: e.activation(
                smc[:, sb8 + 2:sb8 + 3], smc[:, sb8 + 4:sb8 + 5], AF.Exp,
                scale=-0.5))
            w = [(sem["act"], RSG)]
            if tt >= 2:
                w.append((sem[oss], st_done[tt - 2]))
            OD = dve(lambda e, ot=ot, s2=s2, sb8=sb8: e.tensor_scalar(
                ot, s2, smc[:, sb8:sb8 + 1], smc[:, sb8 + 2:sb8 + 3],
                ALU.subtract, ALU.mult), waits=w)
            st_done[tt] = dmaq("sync", oss, lambda e, tt=tt, ot=ot: e.dma_start(
                out=out[tt * 128:(tt + 1) * 128, :], in_=ot),
                waits=[(sem["dve"], OD)])

        for tt in range(TT):
            g_stage_a(tt)
            if tt >= 1:
                g_stage_b(tt - 1)
        g_stage_b(TT - 1)

    waitonly("sync", [(sem[n], sem[n].n) for n in sem])

    with nc.Block() as block:
        block.sync(lambda e: [f(e) for f in sched["sync"]])
        block.tensor(lambda e: [f(e) for f in sched["tensor"]])
        block.vector(lambda e: [f(e) for f in sched["vector"]])
        block.scalar(lambda e: [f(e) for f in sched["scalar"]])
        block.gpsimd(lambda e: [f(e) for f in sched["gpsimd"]])
    nc.finalize()
    return nc


def make_inmaps(inputs, T=2048):
    x = np.ascontiguousarray(np.asarray(inputs["L0_attn_x"], np.float32))
    Wq = np.asarray(inputs["L0_attn_Wq"], np.float32)
    Wk = np.asarray(inputs["L0_attn_Wk"], np.float32)
    Wv = np.asarray(inputs["L0_attn_Wv"], np.float32)
    Wo = np.asarray(inputs["L0_attn_Wo"], np.float32)
    rel = np.asarray(inputs["L0_attn_rel"], np.float32)
    gam = np.asarray(inputs["L0_attn_ln_gamma"], np.float32)
    bet = np.asarray(inputs["L0_attn_ln_beta"], np.float32)
    rw = np.asarray(inputs["L0_ffn_router_w"], np.float32)
    rb = np.asarray(inputs["L0_ffn_router_b"], np.float32)
    Wf = np.asarray(inputs["L0_ffn_W"], np.float32)
    Vf = np.asarray(inputs["L0_ffn_V"], np.float32)
    Wof = np.asarray(inputs["L0_ffn_W_out"], np.float32)

    NGq = H // 2
    wqTg = np.ascontiguousarray(Wq.T.reshape(D, NGq, 128).transpose(1, 0, 2))
    wkTg = np.ascontiguousarray(Wk.T.reshape(D, NGq, 128).transpose(1, 0, 2))
    wvTg = np.ascontiguousarray(Wv.T.reshape(D, NGq, 128).transpose(1, 0, 2))
    woT = np.ascontiguousarray(Wo.T)
    gammac = np.ascontiguousarray(gam.reshape(DC, 128).T)
    betac = np.ascontiguousarray(bet.reshape(DC, 128).T)
    il = np.arange(128)
    dd = np.clip(il[None, :] - il[:, None], 0, 128)
    do = np.clip(128 + il[None, :] - il[:, None], 0, 128)
    biasdT = rel[:, dd].copy()
    biasdT[:, il[None, :] < il[:, None]] = -1e30
    biasoT = rel[:, do].copy()
    rel128b = np.tile(rel[:, 128][None, :], (128, 1))
    rwT = np.ascontiguousarray(rw.T)
    rbb = np.tile(rb[None, :], (128, 1))

    in_maps = []
    for c in range(NCORES):
        in_maps.append(dict(
            xT=np.ascontiguousarray(x[c].T), xtm=x[c],
            wqTg=wqTg, wkTg=wkTg, wvTg=wvTg, woT=woT,
            gammac=gammac, betac=betac,
            biasdT=np.ascontiguousarray(biasdT),
            biasoT=np.ascontiguousarray(biasoT),
            relin=rel, rel128b=np.ascontiguousarray(rel128b),
            rwT=rwT, rbb=np.ascontiguousarray(rbb),
            We=np.ascontiguousarray(Wf[c]), Ve=np.ascontiguousarray(Vf[c]),
            WoutE=np.ascontiguousarray(Wof[c]),
            eidcol=np.full((128, 1), float(c), np.float32),
        ))
    return in_maps


_NC_CACHE = {}


def _nc(part, T=2048):
    key = (part, T)
    if key not in _NC_CACHE:
        _NC_CACHE[key] = build_nc(T=T, CAP=2048, part=part)
    return _NC_CACHE[key]


def kernel(**inputs):
    from concourse.bass_utils import run_bass_kernel_spmd
    T = 2048
    base = make_inmaps(inputs, T=T)
    cores = list(range(NCORES))

    res = run_bass_kernel_spmd(_nc("a", T), base, core_ids=cores)
    ag = [np.asarray(r["ag_in"]) for r in res.results]
    dx = [np.asarray(r["dx_dram"]) for r in res.results]
    lg = [np.asarray(r["lg_in"]) for r in res.results]
    x1_full = np.concatenate(ag, 0)
    lg_full = np.concatenate(lg, 0)

    res = run_bass_kernel_spmd(
        _nc("b", T),
        [dict(m, x1_full=x1_full, lg_full=lg_full) for m in base],
        core_ids=cores)
    Y_full = np.concatenate([np.asarray(r["Y_in"]) for r in res.results], 0)
    ixst = [np.asarray(r["ixst_out"]) for r in res.results]
    kst = [np.asarray(r["kst_out"]) for r in res.results]

    res = run_bass_kernel_spmd(
        _nc("c", T),
        [dict(base[c], Y_full=Y_full, ixst_in=ixst[c], kst_in=kst[c],
              ag_in=ag[c], dx_dram=dx[c]) for c in range(NCORES)],
        core_ids=cores)
    return np.stack([np.asarray(r["out"], np.float32) for r in res.results], 0)
